# version 19 (final)
# baseline (speedup 1.0000x reference)
import numpy as np
import ml_dtypes
from contextlib import ExitStack

import concourse.bass as bass
import concourse.mybir as mybir
from concourse.bass_utils import run_bass_kernel_spmd

F32 = mybir.dt.float32
BF16 = mybir.dt.bfloat16
AF = mybir.ActivationFunctionType
ALU = mybir.AluOpType
AX = mybir.AxisListType
P = 128
BFDT = ml_dtypes.bfloat16


def chunks(total, w=512):
    out, o = [], 0
    while o < total:
        out.append((o, min(w, total - o)))
        o += w
    return out


class CFG:
    def __init__(self, C=2048, T=2048, H=16, F=8192, R=32, NB=4):
        self.C, self.T, self.H, self.F, self.R, self.NB = C, T, H, F, R, NB
        self.DH = 128
        assert C // H == self.DH and C % P == 0 and T % 256 == 0
        self.CT = C // P
        self.TT = T // P
        self.OT = T // 2 // P
        self.OWN = T // 2
        self.FT = F // P
        self.own_chunks = chunks(self.OWN)
        self.all_chunks = chunks(T)



def xpos_tables(cfg, half):
    C, T = cfg.C, cfg.T
    i = np.arange(C // 2, dtype=np.float64)
    dim_scale = (2 * i + 0.4 * C) / (1.4 * C)
    inv_freq = 1.0 / 10000.0 ** (2 * i / C)
    t = np.arange(T, dtype=np.float64)
    scale = dim_scale[None, :] ** (t[:, None] / 512.0)
    sinu = t[:, None] * inv_freq[None, :]
    sinE = np.repeat(np.sin(sinu) * scale, 2, axis=1)
    cosE = np.repeat(np.cos(sinu) * scale, 2, axis=1)
    h_of_c = np.arange(C) // cfg.DH
    lg = np.log1p(-(2.0 ** (-5.0 - h_of_c.astype(np.float64))))
    gq = np.exp(t[:, None] * lg[None, :]) * cfg.DH ** -0.5
    gk = np.exp(-t[:, None] * lg[None, :])
    own = np.arange(cfg.OWN) * 2 + half
    cq = np.ascontiguousarray((cosE[own] * gq[own]).T).astype(np.float32)
    sq = np.ascontiguousarray((sinE[own] * gq[own]).T).astype(np.float32)
    ck = np.ascontiguousarray((cosE * gk).T).astype(np.float32)
    sk = np.ascontiguousarray((sinE * gk).T).astype(np.float32)
    return cq, sq, ck, sk


def host_consts(cfg, half):
    i = np.arange(P)[:, None]
    j = np.arange(256)[None, :]
    keep = (2 * i + half) >= j
    ltri = keep.astype(BFDT)
    jl = np.arange(P)[:, None]
    ii = np.arange(P)[None, :]
    ltriT = np.concatenate([(2 * ii + half >= jl).astype(BFDT),
                            (2 * ii + half >= jl + 128).astype(BFDT)], axis=1)
    eye = np.eye(P, dtype=BFDT)
    pmatT = np.zeros((P, P), np.float32)
    for q in range(P // 2):
        pmatT[2 * q + 1, 2 * q] = -1.0
        pmatT[2 * q, 2 * q + 1] = 1.0
    return ltri, ltriT, eye, pmatT.astype(BFDT)


def cvec(v, cfg):
    return np.ascontiguousarray(np.asarray(v, np.float32).reshape(cfg.CT, P).T)



class Builder:
    def __init__(self, cfg, dbg=()):
        self.cfg = cfg
        self.dbg = set(dbg)
        self.nc = bass.Bass("TRN2", target_bir_lowering=False)
        self.es = ExitStack()
        self.counts = {}
        self.sems = {}
        self.dbg_guard = []
        self.dsems = {}
        self.mem_guard = []

    def sem(self, name):
        h = self.es.enter_context(self.nc.semaphore(name))
        self.counts[name] = 0
        self.sems[name] = h
        return h

    def inc(self, inst, sem, n=1):
        inst.then_inc(sem, n)
        self.counts[sem.name] += n
        return self.counts[sem.name]

    def tok(self, sem):
        return (sem, self.counts[sem.name])

    def wt(self, eng, *toks):
        for sem, v in list(toks) + self.dbg_guard:
            if v > 0:
                eng.wait_ge(sem, v)

    def sbuf(self, name, shape, dtype, es=None):
        return (es or self.es).enter_context(
            self.nc.sbuf_tensor(name, list(shape), dtype))

    def psum(self, name, shape, dtype=F32):
        return self.es.enter_context(self.nc.psum_tensor(name, list(shape), dtype))

    def dsem(self, stream):
        if stream not in self.dsems:
            self.dsems[stream] = self.sem("d_" + stream)
        return self.dsems[stream]

    def dma(self, eng, stream, out, in_):
        sem = self.dsem(stream)
        cnt = self.counts[sem.name]
        if cnt > 0:
            eng.wait_ge(sem, cnt)
        self.inc(eng.dma_start(out=out, in_=in_), sem, 16)
        return self.tok(sem)

    def pump_step(self, n=1):
        for _ in range(n):
            if self.pump:
                self.pump.pop(0)()

    def pump_drain(self):
        while self.pump:
            self.pump.pop(0)()

    def T(self, fn): fn(self.nc.tensor)
    def V(self, fn): fn(self.nc.vector)
    def S(self, fn): fn(self.nc.scalar)
    def G(self, fn): fn(self.nc.gpsimd)
    def Y(self, fn): fn(self.nc.sync)

    def build(self):
        cfg = self.cfg
        nc = self.nc
        C, T, H, R, F = cfg.C, cfg.T, cfg.H, cfg.R, cfg.F
        CT, TT, OT, OWN, FT = cfg.CT, cfg.TT, cfg.OT, cfg.OWN, cfg.FT
        self.out_names = ["out_own"]

        din = lambda n, s: nc.dram_tensor(n, s, F32, kind="ExternalInput").ap()
        dbf = lambda n, s: nc.dram_tensor(n, s, BF16, kind="ExternalInput").ap()
        dint = lambda n, s: nc.dram_tensor(n, s, BF16, kind="Internal").ap()

        x_all = din("x_all", [T, C])
        x_own = din("x_own", [OWN, C])
        w_qkv = din("w_qkv", [C, 3 * C])
        w_g = din("w_g", [C, C])
        w_proj = din("w_proj", [C, C])
        wqkvT_f = din("wqkvT_f", [C, 3 * C])
        woT = din("woT", [C, C])
        ff1T = din("ff1T", [C, F])
        ff2T = din("ff2T", [F, C])
        w_downT = din("w_downT", [C, R])
        w_upT = din("w_upT", [R, C])
        wrT = din("wrT", [C, 3])
        cvecs = din("cvecs", [P, 7 * CT])
        cq_d = din("cq", [C, OWN]); sq_d = din("sq", [C, OWN])
        ck_d = din("ck", [C, T]); sk_d = din("sk", [C, T])
        ltri_d = dbf("ltri", [P, 256])
        ltriT_d = dbf("ltriT", [P, 256])
        eye_d = dbf("eye", [P, P])
        pmatT_d = dbf("pmatT", [P, P])
        out_own = nc.dram_tensor("out_own", [OWN, C], F32, kind="ExternalOutput").ap()
        yn_sp = dint("yn_sp", [C, OWN])
        s_sp = dint("s_sp", [C, OWN])
        ctx_sp = dint("ctx_sp", [C, OWN])
        x2_sp = dint("x2_sp", [C, OWN])
        hg_sp = dint("hg_sp", [F, OWN])

        ff1T_bf = dint("ff1T_bf", [C, F])
        ff2T_bf = dint("ff2T_bf", [F, C])

        r3 = lambda ap: ap.rearrange("(a p) n -> p a n", p=P)
        self.wqkv_r = r3(w_qkv)
        self.wg_r = r3(w_g)
        self.wproj_r = r3(w_proj)
        self.wqkvTf_r = r3(wqkvT_f)
        self.woT_r = r3(woT)
        self.ff1T_r = r3(ff1T)
        self.ff2T_r = r3(ff2T)
        self.ff1Tbf_r = r3(ff1T_bf)
        self.ff2Tbf_r = r3(ff2T_bf)
        wdT_r = r3(w_downT)
        wrT_r = r3(wrT)
        xall_r = x_all.rearrange("(tt p) c -> tt p c", p=P)
        xown_r = x_own.rearrange("(rt p) c -> rt p c", p=P)
        outown_r = out_own.rearrange("(rt p) c -> rt p c", p=P)
        slab = lambda ap: ap.rearrange("(h p) t -> h p t", p=P)
        self.cq_r, self.sq_r = slab(cq_d), slab(sq_d)
        self.ck_r, self.sk_r = slab(ck_d), slab(sk_d)
        self.yn_sp_r = slab(yn_sp)
        self.s_sp_r = slab(s_sp)
        self.ctx_sp_r = slab(ctx_sp)
        self.x2_sp_r = slab(x2_sp)
        hg_sp_w = hg_sp.rearrange("(ft p) t -> ft p t", p=P)
        hg_sp_r = hg_sp.rearrange("(ft p) t -> p ft t", p=P)

        sqd = self.sem("sqd"); sgd = self.sem("sgd")
        s_t = self.sem("s_t"); s_v = self.sem("s_v")
        s_s = self.sem("s_s"); s_p = self.sem("s_p")
        self.sqd, self.sgd, self.s_t, self.s_v, self.s_s, self.s_p = \
            sqd, sgd, s_t, s_v, s_s, s_p

        eye_sb = self.sbuf("eye_sb", [P, P], BF16)
        pmatT_sb = self.sbuf("pmatT_sb", [P, P], BF16)
        ltri_sb = self.sbuf("ltri_sb", [P, 256], BF16)
        ltriT_sb = self.sbuf("ltriT_sb", [P, 256], BF16)
        self.ltriT_sb = ltriT_sb
        ones_col = self.sbuf("ones_col", [P, 1], BF16)
        self.ones_row_f_early = self.sbuf("ones_row_f", [P, P], F32)
        self.ones_row_f = self.ones_row_f_early
        cv_sb = self.sbuf("cv_sb", [P, 7 * CT], F32)
        wrT_sb = self.sbuf("wrT_sb", [P, CT * 3], F32)
        wrhi = self.sbuf("wrhi", [P, CT * 3], BF16)
        wrlo = self.sbuf("wrlo", [P, CT * 3], BF16)
        wdT_sb = self.sbuf("wdT_sb", [P, CT * R], BF16)
        xn_ownT = self.sbuf("xn_ownT", [P, CT, OWN], BF16)
        xn_allT = self.sbuf("xn_allT", [P, CT, T], BF16)
        masks_bf = self.sbuf("masks_bf", [1, 2 * OWN], BF16)
        self.masks_bf = masks_bf
        ones_row_bf = self.sbuf("ones_row_bf", [1, P], BF16)
        self.ones_row_bf = ones_row_bf
        lr_sb = self.sbuf("lr_sb", [R, OWN], BF16)
        self.lr_sb = lr_sb
        self.eye_sb, self.pmatT_sb, self.ltri_sb = eye_sb, pmatT_sb, ltri_sb
        self.ones_col = ones_col
        self.xn_ownT, self.xn_allT = xn_ownT, xn_allT
        cvv = {nm: cv_sb[:, k * CT:(k + 1) * CT]
               for k, nm in enumerate(["gnw", "gnb", "ln1s", "ln1b", "ln2s", "ln2b", "sbias"])}

        pbA = self.psum("pbA", [P, 512]); pbB = self.psum("pbB", [P, 512])
        pbC = self.psum("pbC", [P, 512]); pbD = self.psum("pbD", [P, 512])
        pbY = self.psum("pbY", [P, 512]); pbS = self.psum("pbS", [P, 512])
        pbT = self.psum("pbT", [P, 1024], BF16)
        pbT2 = self.psum("pbT2", [P, 1024], BF16)
        self.pb = dict(A=pbA, B=pbB, C=pbC, D=pbD, Y=pbY, S=pbS, T=pbT, T2=pbT2)
        self.bfree = {k: [] for k in self.pb}

        self.block = self.es.enter_context(nc.Block())

        def _emit_all(_carrier_engine):
            def _(s):
                self.dma(s, "consts", ltri_sb[:], ltri_d[:])
                self.dma(s, "consts", ltriT_sb[:], ltriT_d[:])
                self.dma(s, "consts", eye_sb[:], eye_d[:])
                self.dma(s, "consts", pmatT_sb[:], pmatT_d[:])
                self.dma(s, "consts", cv_sb[:], cvecs[:])
                self.dma(s, "consts", wrT_sb.ap().rearrange("p (a n) -> p a n", n=3),
                         wrT_r[:, :, :])
            self.Y(_)
            consts_sq = self.tok(self.dsem("consts"))

            def _(g):
                g.memset(ones_col[:], 1.0)
                g.memset(self.ones_row_f_early[:], 1.0)
                g.memset(ones_row_bf[:], 1.0)
                self.inc(g.memset(wrhi[:], 0.0), s_p)
                self.dma(g, "gconsts", wdT_sb.ap().rearrange("p (a n) -> p a n", n=R),
                         wdT_r[:, :, :])
            self.G(_)
            consts_sg = self.tok(self.dsem("gconsts"))
            consts_sp = self.tok(s_p)
            self.consts_sq, self.consts_sg, self.consts_sp = consts_sq, consts_sg, consts_sp

            self.pc = {}
            self.pump = []

            def _mkcast(stream, dst, srcw, r0, r1, key):
                def go():
                    def _(g):
                        self.dma(g, stream, dst[r0:r1, :], srcw[r0:r1, :])
                    self.G(_)
                    if key is not None:
                        self.pc[key] = self.tok(self.dsem(stream))
                return go

            def _fill(stream, mir, srcw, key, nch):
                K = mir.shape[0]
                nch = min(nch, K)
                bounds = [K * i // nch for i in range(nch + 1)]
                for i in range(nch):
                    self.pump.append(_mkcast(stream, mir, srcw,
                                             bounds[i], bounds[i + 1],
                                             key if i == nch - 1 else None))
            _fill("pc1", ff1T_bf, ff1T, "ff1", 8)
            _fill("pc1", ff2T_bf, ff2T, "ff2", 8)

            def _(v):
                self.wt(v, consts_sq, consts_sp)
                self.inc(v.tensor_copy(wrhi[:], wrT_sb[:]), s_v)
                self.wt(v, self.tok(s_v))
                self.inc(v.tensor_tensor(wrlo[:], wrT_sb[:], wrhi[:], op=ALU.subtract), s_v)
            self.V(_)
            wr_tok = self.tok(s_v)

            st1 = self.sbuf("p0s1", [P, 1], F32)
            st2 = self.sbuf("p0s2", [P, 1], F32)
            stm = self.sbuf("p0sm", [P, 1], F32)
            ph0_es = ExitStack()
            xa = [self.sbuf(f"p0x{i}", [P, C], F32, es=ph0_es) for i in range(2)]
            xn_tok = [self.sbuf(f"p0n{i}", [P, C], BF16, es=ph0_es) for i in range(2)]
            lo_tok = [self.sbuf(f"p0l{i}", [P, C], BF16, es=ph0_es) for i in range(2)]
            xnloT = self.sbuf("xnloT", [P, CT, OWN], BF16, es=ph0_es)
            l_sb = self.sbuf("l_sb", [1, 3 * OWN], F32, es=ph0_es)
            mtmp = self.sbuf("mtmp", [1, 2 * OWN], F32, es=ph0_es)

            buf_guard = [[], []]
            self.ln_guard = []

            def ln_tile(kind, idx):
                buf = idx % 2
                xt = xa[buf]

                ldtok_box = []

                def _(s):
                    self.wt(s, *buf_guard[buf])
                    src = xall_r[idx] if kind == "all" else xown_r[idx]
                    ldtok_box.append(self.dma(s, f"xload{buf}", xt[:], src))
                self.Y(_)
                ldtok = ldtok_box[0]

                def _(v):
                    self.wt(v, ldtok, *self.ln_guard)
                    self.inc(v.tensor_reduce(st1[:], xt[:], axis=AX.X, op=ALU.add), s_v)
                self.V(_)
                t_mean = self.tok(s_v)

                def _(s):
                    self.wt(s, ldtok, *self.ln_guard)
                    self.inc(s.activation(lo_tok[buf][:], xt[:], AF.Square,
                                          accum_out=st2[:]), s_s)
                self.S(_)
                t_sq = self.tok(s_s)

                def _(v):
                    self.wt(v, t_mean, t_sq)
                    self.inc(v.tensor_scalar(stm[:], st1[:], 1.0 / C, None, op0=ALU.mult), s_v)
                    self.wt(v, self.tok(s_v))
                    self.inc(v.tensor_tensor(st1[:], stm[:], stm[:], op=ALU.mult), s_v)
                    self.wt(v, self.tok(s_v))
                    self.inc(v.tensor_scalar(st2[:], st2[:], 1.0 / C, None, op0=ALU.mult), s_v)
                    self.wt(v, self.tok(s_v))
                    self.inc(v.tensor_tensor(st2[:], st2[:], st1[:], op=ALU.subtract), s_v)
                    self.wt(v, self.tok(s_v))
                    self.inc(v.tensor_scalar(st2[:], st2[:], 1e-5, None, op0=ALU.add), s_v)
                self.V(_)
                t_rv = self.tok(s_v)

                def _(s):
                    self.wt(s, t_rv)
                    self.inc(s.activation(st1[:], st2[:], AF.Ln), s_s)
                    self.wt(s, self.tok(s_s))
                    self.inc(s.activation(st2[:], st1[:], AF.Exp, scale=-0.5), s_s)
                self.S(_)
                t_rstd = self.tok(s_s)

                def _(v):
                    self.wt(v, t_rstd)
                    self.inc(v.tensor_scalar(xn_tok[buf][:], xt[:], stm[:], st2[:],
                                             op0=ALU.subtract, op1=ALU.mult), s_v)
                    if kind == "own":
                        self.inc(v.tensor_scalar(xt[:], xt[:], stm[:], st2[:],
                                                 op0=ALU.subtract, op1=ALU.mult), s_v)
                        self.wt(v, self.tok(s_v))
                        self.inc(v.tensor_tensor(lo_tok[buf][:], xt[:], xn_tok[buf][:],
                                                 op=ALU.subtract), s_v)
                self.V(_)
                self.ln_guard = [self.tok(s_v)]
                return self.tok(s_v)

            tpar = [0]

            def tp_tile(src, t_ready, dest, dw, col):
                toks = []
                for g in range((CT + 7) // 8):
                    nk = min(8, CT - g * 8)
                    key = "T" if tpar[0] % 2 == 0 else "T2"
                    tpar[0] += 1
                    pb = self.pb[key]

                    def _(t, g=g, nk=nk, pb=pb, key=key):
                        self.wt(t, t_ready, self.consts_sq, *self.bfree[key])
                        for k in range(nk):
                            ct = g * 8 + k
                            inst = t.transpose(pb[:, k * P:(k + 1) * P],
                                               src[:, ct * P:(ct + 1) * P], eye_sb[:])
                        self.inc(inst, s_t)
                    self.T(_)
                    tg = self.tok(s_t)

                    def _(v, g=g, nk=nk, pb=pb, key=key):
                        self.wt(v, tg)
                        for k in range(nk):
                            ct = g * 8 + k
                            iv = v.tensor_copy(dest[:, ct, col:col + P], pb[:, k * P:(k + 1) * P])
                        self.inc(iv, s_v)
                    self.V(_)
                    self.bfree[key] = [self.tok(s_v)]
                    toks.append(self.tok(s_v))
                return toks[-1]

            for idx in range(TT):
                t_r = ln_tile("all", idx)
                e = tp_tile(xn_tok[idx % 2], t_r, xn_allT, T, idx * P)
                buf_guard[idx % 2] = [e]
            for idx in range(OT):
                t_r = ln_tile("own", idx)
                e1 = tp_tile(xn_tok[idx % 2], t_r, xn_ownT, OWN, idx * P)
                e2 = tp_tile(lo_tok[idx % 2], t_r, xnloT, OWN, idx * P)
                buf_guard[idx % 2] = [e1, e2]
            ph0_v = self.tok(s_v)
            self.dbg_dump("xn_allT", xn_allT, [P, CT * T], [ph0_v])
            self.dbg_dump("xn_ownT", xn_ownT, [P, CT * OWN], [ph0_v])

            nprod = CT * 4
            for j in range(3):
                for (c0, cw) in cfg.own_chunks:
                    def _(t, j=j, c0=c0, cw=cw):
                        self.wt(t, ph0_v, wr_tok, *self.bfree["S"])
                        k = 0
                        for ct in range(CT):
                            for wsb, xsb in ((wrhi, xn_ownT), (wrhi, xnloT),
                                             (wrlo, xn_ownT), (wrlo, xnloT)):
                                inst = t.matmul(pbS[0:1, 0:cw],
                                                wsb[:, ct * 3 + j:ct * 3 + j + 1],
                                                xsb[:, ct, c0:c0 + cw],
                                                start=(k == 0), stop=(k == nprod - 1))
                                k += 1
                        self.inc(inst, s_t)
                    self.T(_)
                    tg = self.tok(s_t)

                    def _(v, j=j, c0=c0, cw=cw, tg=tg):
                        self.wt(v, tg)
                        self.inc(v.tensor_copy(l_sb[0:1, j * OWN + c0:j * OWN + c0 + cw],
                                               pbS[0:1, 0:cw]), s_v)
                    self.V(_)
                    self.bfree["S"] = [self.tok(s_v)]

            l0 = l_sb[0:1, 0 * OWN:1 * OWN]
            l1 = l_sb[0:1, 1 * OWN:2 * OWN]
            l2 = l_sb[0:1, 2 * OWN:3 * OWN]
            m0 = masks_bf[0:1, 0 * OWN:1 * OWN]
            m1 = masks_bf[0:1, 1 * OWN:2 * OWN]
            ta = mtmp[0:1, 0:OWN]
            tb = mtmp[0:1, OWN:2 * OWN]

            def _(v):
                self.wt(v, self.tok(s_v))
                self.inc(v.tensor_tensor(ta, l0, l1, op=ALU.is_ge), s_v)
                self.inc(v.tensor_tensor(tb, l0, l2, op=ALU.is_ge), s_v)
                self.wt(v, self.tok(s_v))
                self.inc(v.tensor_tensor(m0, ta, tb, op=ALU.mult), s_v)
                self.wt(v, self.tok(s_v))
                self.inc(v.tensor_tensor(ta, l1, l2, op=ALU.is_ge), s_v)
                self.wt(v, self.tok(s_v))
                self.inc(v.tensor_scalar(tb, m0, -1.0, 1.0, op0=ALU.mult, op1=ALU.add), s_v)
                self.wt(v, self.tok(s_v))
                self.inc(v.tensor_tensor(m1, tb, ta, op=ALU.mult), s_v)
            self.V(_)
            self.masks_tok = self.tok(s_v)
            self.m0, self.m1 = m0, m1
            if "masks" in self.dbg:
                mdump = self.nc.dram_tensor("dbg_masks", [1, 2 * OWN], F32,
                                            kind="ExternalOutput").ap()
                self.out_names.append("dbg_masks")

                def _(g):
                    self.wt(g, self.masks_tok)
                    self.dbg_tok = self.dma(g, "dbg", mdump[:], masks_bf[0:1, 0:2 * OWN])
                self.G(_)
                self.dbg_guard = [self.dbg_tok]

            for (c0, cw) in cfg.own_chunks:
                def _(t, c0=c0, cw=cw):
                    self.wt(t, ph0_v, consts_sg, *self.bfree["S"])
                    for ct in range(CT):
                        inst = t.matmul(pbS[0:R, 0:cw], wdT_sb[:, ct * R:(ct + 1) * R],
                                        xn_ownT[:, ct, c0:c0 + cw],
                                        start=(ct == 0), stop=(ct == CT - 1))
                    self.inc(inst, s_t)
                self.T(_)
                tg = self.tok(s_t)

                def _(v, c0=c0, cw=cw, tg=tg):
                    self.wt(v, tg)
                    self.inc(v.tensor_copy(lr_sb[:, c0:c0 + cw], pbS[0:R, 0:cw]), s_v)
                self.V(_)
                self.bfree["S"] = [self.tok(s_v)]
            self.lr_tok = self.tok(s_v)

            ph0_end = (self.tok(s_t), self.tok(s_v), self.tok(s_s))
            ph0_es.close()
            self.attn_es = ExitStack()
            self.tab_es = ExitStack()
            self.mk_attn_bufs()
            self.cvv = cvv
            self.mem_guard = list(ph0_end)
            self.emit_attention("ret", xn_ownT, yn_sp=self.yn_sp_r, ready=[ph0_v])
            self.dbg_dump("yn", None, [C, OWN], [], dram=yn_sp,
                          guards_dma=[self.attn_done[3]])

            ret_end = (self.tok(s_t), self.tok(s_v), self.tok(s_s))
            self.tab_es.close()
            self.bufA_es = ExitStack()
            bufA = self.sbuf("bufA", [P, CT, OWN], BF16, es=self.bufA_es)
            self.bufA = bufA
            self.mem_guard = list(ret_end)

            self.row_es = ExitStack()
            self.rowbuf = self.sbuf("rowbuf", [P, 3 * OWN], F32, es=self.row_es)
            self.qn_row = self.rowbuf[0:1, 0:OWN]
            self.nc_row = self.rowbuf[0:1, OWN:2 * OWN]

            ln_ready = list(ret_end)
            late_ready = []
            sa0, ta0 = self.ln_units(xn_allT, xn_allT, 0, OWN, cvv["ln1s"],
                                     cvv["ln1b"], ln_ready,
                                     scr_pair=(self.q2c, self.k2c),
                                     rowp=0)
            sa1, ta1 = self.ln_units(xn_allT, xn_allT, OWN, T - OWN, cvv["ln1s"],
                                     cvv["ln1b"], ln_ready,
                                     scr_pair=(self.q2c, self.k2c),
                                     rowp=32)
            so, to = self.ln_units(xn_ownT, bufA, 0, OWN, cvv["ln1s"],
                                   cvv["ln1b"], ln_ready,
                                   scr_pair=(self.q2c, self.k2c),
                                   apply_ready=late_ready, rowp=64)
            self.emit_gate_proj(cvv, fill_gate=sa0 + sa1 + so,
                                fill_proj=ta0 + ta1)
            self.dbg_dump("s", None, [C, OWN], [], dram=s_sp, guards_dma=list(self.proj_sqd))
            ln1a_tok = self._ln_tok
            late_ready.append(self.proj_tok)
            for fn in to:
                fn()
            ln1o_tok = self._ln_tok
            self.dbg_dump("xn1_allT", xn_allT, [P, CT * T], [ln1a_tok])
            self.dbg_dump("xn1_ownT", bufA, [P, CT * OWN], [ln1o_tok])

            self.emit_attention("ffn", bufA, ctx_sp=self.ctx_sp_r,
                                ready=[ln1a_tok, ln1o_tok])
            self.pump_drain()
            self.dbg_dump("ctx", None, [C, OWN], [], dram=ctx_sp,
                          guards_dma=[self.attn_done[3]])

            self.emit_wo_x2_ln2(cvv)

            self.emit_ff_stageA(hg_sp_w)

            self.dbg_dump("hg", None, [F, OWN], [], dram=hg_sp,
                          guards_dma=[self.tok(self.dsem("hgsp0")),
                                      self.tok(self.dsem("hgsp1"))])
            stA_end = (self.tok(s_t), self.tok(s_v), self.tok(s_s))
            self.row_es.close()
            self.bufA_es.close()
            self.attn_es.close()
            self.mem_guard = list(stA_end) + list(self.stageA_sqd)
            self.w_upT_d = w_upT
            self.emit_stageB_combine(hg_sp_r, cvv, xown_r, outown_r)

            def _(s):
                for nm, sem in self.dsems.items():
                    self.wt(s, self.tok(sem))
            self.Y(_)

        self.block.sync(_emit_all)
        self.sb_es.close()
        self.es.close()
        return self.nc

    def dbg_dump(self, name, sb, shape, guards, dram=None, guards_dma=()):
        if name not in self.dbg:
            return
        ap = self.nc.dram_tensor("dbg_" + name, list(shape), F32, kind="ExternalOutput").ap()
        self.out_names.append("dbg_" + name)
        if dram is not None:
            src = dram[:]
        else:
            src = sb.ap()
            if len(src.shape) == 3:
                src = src.rearrange("p a n -> p (a n)")
            else:
                src = sb[:]

        def _(g):
            for sem, v in list(guards) + list(guards_dma):
                if v > 0:
                    g.wait_ge(sem, v)
            self.dbg_tok = self.dma(g, "dbg", ap[:], src)
        self.G(_)
        self.dbg_guard = [self.dbg_tok]

    def mk_attn_bufs(self):
        cfg = self.cfg
        OWN, T, OT = cfg.OWN, cfg.T, cfg.OT
        aes, tes = self.attn_es, self.tab_es
        self.wq_h = [self.sbuf(f"wqh{i}", [P, cfg.CT, P], BF16, es=aes) for i in range(2)]
        self.wk_h = [self.sbuf(f"wkh{i}", [P, cfg.CT, P], BF16, es=aes) for i in range(2)]
        self.wv_h = [self.sbuf(f"wvh{i}", [P, cfg.CT, P], BF16, es=aes) for i in range(2)]
        self.qb_sb = self.sbuf("qb_sb", [P, OWN], BF16, es=aes)
        self.q2c = self.sbuf("q2c", [P, 512], BF16, es=aes)
        self.kb_sb = self.sbuf("kb_sb", [P, T], BF16, es=aes)
        self.k2c = self.sbuf("k2c", [P, 512], BF16, es=aes)
        self.vtmp = self.sbuf("vtmp", [P, 512], BF16, es=aes)
        self.vtmp2 = self.sbuf("vtmp2", [P, 512], BF16, es=aes)
        self.vtok = self.sbuf("vtok", [P, T], BF16, es=aes)
        self.sc_pair = [self.sbuf(f"scp{i}", [P, T], BF16, es=aes) for i in range(2)]
        self.scT_sb = self.sbuf("scT_sb", [P, T], BF16, es=aes)
        self.wt_pair = [self.sbuf(f"wtp{i}", [P, 256], BF16, es=aes) for i in range(2)]
        self.yslab = self.sbuf("yslab", [P, OWN], BF16, es=aes)
        self.ybuf = self.sbuf("ybuf", [P, OT * P], BF16, es=aes)
        self.ybuf2 = self.scT_sb[:, 0:OT * P]
        self.gstA = self.sbuf("gstA", [P, OT], F32, es=aes)
        self.gstB = self.sbuf("gstB", [P, OT], F32, es=aes)
        self.gstC = self.sbuf("gstC", [P, OT], F32, es=aes)
        self.den_sb = self.sbuf("den_sb", [P, 16], F32, es=aes)
        self.denall = self.sbuf("denall", [P, OT], F32, es=aes)
        self.mk2 = self.sbuf("mk2", [1, 1], F32, es=aes)
        self.mk2b = self.sbuf("mk2b", [1, 1], F32, es=aes)
        self.biasT = self.sbuf("biasT", [P, OT], F32, es=aes)
        self.sqtmp = self.vtmp
        self.lntmp = self.sbuf("lntmp", [P, 512], BF16, es=aes)

        self.qq_sb = self.sbuf("qq_sb", [P, OWN], BF16, es=tes)
        self.cq_sb = [self.sbuf(f"cqs{i}", [P, OWN], BF16, es=tes) for i in range(2)]
        self.sq_sb = [self.sbuf(f"sqs{i}", [P, OWN], BF16, es=tes) for i in range(2)]
        self.ck_sb = [self.sbuf(f"cks{i}", [P, T], BF16, es=tes) for i in range(2)]
        self.sk_sb = [self.sbuf(f"sks{i}", [P, T], BF16, es=tes) for i in range(2)]
        self.kk_sb = self.sbuf("kk_sb", [P, T], BF16, es=tes)
        self.g_sqtmp = []
        self.g_lntmp = []
        self.g_sil = []
        self.g_w2 = [[], []]
        self.g_w = [[], []]
        self.g_tab = [[], []]
        self.g_q = []
        self.g_k = []
        self.g_v = []
        self.g_scp = [[], []]
        self.g_wtp = [[], []]
        self.g_denp = [[], []]
        self.g_scT = []
        self.g_ybuf = []
        self.g_yb2 = []
        self.g_denall = []
        self.g_yslab = []
        self.g_bias = []

    def load_head(self, h, kind):
        cfg = self.cfg
        buf = h % 2
        C = cfg.C
        wsrc = self.wqkv_r if kind == "ret" else self.wqkvTf_r

        def _(g, h=h, buf=buf):
            self.wt(g, *self.g_w[buf], *self.g_w2[buf], *self.mem_guard)
            self.dma(g, f"wh{buf}", self.wq_h[buf].ap(),
                     wsrc[:, :, 0 * C + h * P:0 * C + (h + 1) * P])
            self.dma(g, f"wh{buf}", self.wk_h[buf].ap(),
                     wsrc[:, :, 1 * C + h * P:1 * C + (h + 1) * P])
            self.dma(g, f"wh{buf}", self.wv_h[buf].ap(),
                     wsrc[:, :, 2 * C + h * P:2 * C + (h + 1) * P])
        self.G(_)
        wtok = self.tok(self.dsem(f"wh{buf}"))
        ttok = None
        if kind == "ret":
            def _(g, h=h, buf=buf):
                self.wt(g, *self.g_tab[buf], *self.mem_guard)
                self.dma(g, f"tab{buf}", self.cq_sb[buf][:], self.cq_r[h])
                self.dma(g, f"tab{buf}", self.sq_sb[buf][:], self.sq_r[h])
                self.dma(g, f"tab{buf}", self.ck_sb[buf][:], self.ck_r[h])
                self.dma(g, f"tab{buf}", self.sk_sb[buf][:], self.sk_r[h])
            self.G(_)
            ttok = self.tok(self.dsem(f"tab{buf}"))
        self.wload[h] = wtok
        self.tload[h] = ttok

    def emit_attention(self, kind, xown3, yn_sp=None, ctx_sp=None, ready=()):
        cfg = self.cfg
        CT, OT, OWN, T, H = cfg.CT, cfg.OT, cfg.OWN, cfg.T, cfg.H
        s_t, s_v, s_s = self.s_t, self.s_v, self.s_s
        pb = self.pb
        qscale = 1.0 if kind == "ret" else cfg.DH ** -0.5
        spill = yn_sp if kind == "ret" else ctx_sp
        self.g_den = []
        self.g_rden = []

        self.wload = [None] * H
        self.tload = [None] * H
        self.load_head(0, kind)

        for h in range(H):
            buf = h % 2
            if h + 1 < H:
                self.load_head(h + 1, kind)
            if kind == "ffn":
                self.pump_step()

            for ci, (c0, cw) in enumerate(cfg.own_chunks):
                key = "A" if ci % 2 == 0 else "B"

                def _(t, c0=c0, cw=cw, key=key, buf=buf):
                    self.wt(t, self.wload[h], *ready, *self.bfree[key], *self.g_q)
                    for ct in range(CT):
                        inst = t.matmul(pb[key][:, 0:cw], self.wq_h[buf][:, ct, :],
                                        xown3[:, ct, c0:c0 + cw],
                                        start=(ct == 0), stop=(ct == CT - 1))
                    self.inc(inst, s_t)
                self.T(_)
                tg = self.tok(s_t)
                if ci % 2 == 0 or kind == "ffn":
                    def _(s, c0=c0, cw=cw, key=key, tg=tg):
                        self.wt(s, tg)
                        self.inc(s.activation(self.qb_sb[:, c0:c0 + cw], pb[key][:, 0:cw],
                                              AF.Identity, scale=qscale), s_s)
                    self.S(_)
                    self.bfree[key] = [self.tok(s_s)]
                else:
                    def _(v, c0=c0, cw=cw, key=key, tg=tg):
                        self.wt(v, tg)
                        self.inc(v.tensor_scalar(self.qb_sb[:, c0:c0 + cw],
                                                 pb[key][:, 0:cw], qscale, None,
                                                 op0=ALU.mult), s_v)
                    self.V(_)
                    self.bfree[key] = [self.tok(s_v)]
            q_ev = (self.tok(s_s), self.tok(s_v))
            self.g_q = []

            if kind == "ret":
                for ci, (c0, cw) in enumerate(cfg.own_chunks):
                    key = "C" if ci % 2 == 0 else "D"

                    def _(t, c0=c0, cw=cw, key=key):
                        self.wt(t, *q_ev, *self.bfree[key])
                        inst = t.matmul(pb[key][:, 0:cw], self.pmatT_sb[:],
                                        self.qb_sb[:, c0:c0 + cw], start=True, stop=True)
                        self.inc(inst, s_t)
                    self.T(_)
                    tg = self.tok(s_t)

                    def _(v, c0=c0, cw=cw, key=key, tg=tg, buf=buf):
                        self.wt(v, tg, self.tload[h], self.tok(s_v))
                        self.inc(v.tensor_tensor(self.qq_sb[:, c0:c0 + cw],
                                                 self.qb_sb[:, c0:c0 + cw],
                                                 self.cq_sb[buf][:, c0:c0 + cw], op=ALU.mult), s_v)
                        self.inc(v.tensor_tensor(self.q2c[:, 0:cw], pb[key][:, 0:cw],
                                                 self.sq_sb[buf][:, c0:c0 + cw], op=ALU.mult), s_v)
                        self.wt(v, self.tok(s_v))
                        self.inc(v.tensor_tensor(self.qq_sb[:, c0:c0 + cw],
                                                 self.qq_sb[:, c0:c0 + cw],
                                                 self.q2c[:, 0:cw], op=ALU.add), s_v)
                    self.V(_)
                    self.bfree[key] = [self.tok(s_v)]
                q_src, q_done = self.qq_sb, [self.tok(s_v)]
            else:
                q_src, q_done = self.qb_sb, list(q_ev)

            for ci, (c0, cw) in enumerate(cfg.all_chunks):
                key = "A" if ci % 2 == 0 else "B"

                def _(t, c0=c0, cw=cw, key=key, buf=buf):
                    self.wt(t, self.wload[h], *ready, *self.bfree[key], *self.g_k)
                    for ct in range(CT):
                        inst = t.matmul(pb[key][:, 0:cw], self.wk_h[buf][:, ct, :],
                                        self.xn_allT[:, ct, c0:c0 + cw],
                                        start=(ct == 0), stop=(ct == CT - 1))
                    self.inc(inst, s_t)
                self.T(_)
                tg = self.tok(s_t)
                if ci % 2 == 0:
                    def _(s, c0=c0, cw=cw, key=key, tg=tg):
                        self.wt(s, tg)
                        self.inc(s.activation(self.kb_sb[:, c0:c0 + cw], pb[key][:, 0:cw],
                                              AF.Identity), s_s)
                    self.S(_)
                    self.bfree[key] = [self.tok(s_s)]
                else:
                    def _(v, c0=c0, cw=cw, key=key, tg=tg):
                        self.wt(v, tg)
                        self.inc(v.tensor_copy(self.kb_sb[:, c0:c0 + cw],
                                               pb[key][:, 0:cw]), s_v)
                    self.V(_)
                    self.bfree[key] = [self.tok(s_v)]
            k_ev = (self.tok(s_s), self.tok(s_v))
            self.g_k = []
            if kind == "ret":
                for ci, (c0, cw) in enumerate(cfg.all_chunks):
                    key = "C" if ci % 2 == 0 else "D"

                    def _(t, c0=c0, cw=cw, key=key):
                        self.wt(t, *k_ev, *self.bfree[key])
                        inst = t.matmul(pb[key][:, 0:cw], self.pmatT_sb[:],
                                        self.kb_sb[:, c0:c0 + cw], start=True, stop=True)
                        self.inc(inst, s_t)
                    self.T(_)
                    tg = self.tok(s_t)

                    def _(v, c0=c0, cw=cw, key=key, tg=tg, buf=buf):
                        self.wt(v, tg, self.tload[h], self.tok(s_v))
                        self.inc(v.tensor_tensor(self.kk_sb[:, c0:c0 + cw],
                                                 self.kb_sb[:, c0:c0 + cw],
                                                 self.ck_sb[buf][:, c0:c0 + cw], op=ALU.mult), s_v)
                        self.inc(v.tensor_tensor(self.k2c[:, 0:cw], pb[key][:, 0:cw],
                                                 self.sk_sb[buf][:, c0:c0 + cw], op=ALU.mult), s_v)
                        self.wt(v, self.tok(s_v))
                        self.inc(v.tensor_tensor(self.kk_sb[:, c0:c0 + cw],
                                                 self.kk_sb[:, c0:c0 + cw],
                                                 self.k2c[:, 0:cw], op=ALU.add), s_v)
                    self.V(_)
                    self.bfree[key] = [self.tok(s_v)]
                k_src, k_done = self.kk_sb, [self.tok(s_v)]
            else:
                k_src, k_done = self.kb_sb, list(k_ev)

            vg = []
            for ci, (c0, cw) in enumerate(cfg.all_chunks):
                key = "A" if ci % 2 == 0 else "B"
                vt = self.vtmp if ci % 2 == 0 else self.vtmp2

                def _(t, c0=c0, cw=cw, key=key, buf=buf):
                    self.wt(t, self.wload[h], *ready, *self.bfree[key], *self.g_v)
                    for ct in range(CT):
                        inst = t.matmul(pb[key][:, 0:cw], self.wv_h[buf][:, ct, :],
                                        self.xn_allT[:, ct, c0:c0 + cw],
                                        start=(ct == 0), stop=(ct == CT - 1))
                    self.inc(inst, s_t)
                self.T(_)
                tg = self.tok(s_t)

                def _(s, cw=cw, key=key, tg=tg, vt=vt, vgl=list(vg)):
                    self.wt(s, tg, *vgl)
                    self.inc(s.activation(vt[:, 0:cw], pb[key][:, 0:cw], AF.Identity), s_s)
                self.S(_)
                ev = self.tok(s_s)
                self.bfree[key] = [ev]
                tkey = "T" if ci % 2 == 0 else "T2"

                def _(t, c0=c0, cw=cw, tkey=tkey, ev=ev, vt=vt):
                    self.wt(t, ev, *self.bfree[tkey])
                    for k in range(cw // P):
                        inst = t.transpose(pb[tkey][:, k * P:(k + 1) * P],
                                           vt[:, k * P:(k + 1) * P], self.eye_sb[:])
                    self.inc(inst, s_t)
                self.T(_)
                tg2 = self.tok(s_t)

                def _(v, c0=c0, cw=cw, tkey=tkey, tg2=tg2):
                    self.wt(v, tg2)
                    self.inc(v.tensor_copy(self.vtok[:, c0:c0 + cw],
                                           pb[tkey][:, 0:cw]), s_v)
                self.V(_)
                self.bfree[tkey] = [self.tok(s_v)]
                vg = [self.tok(s_v)]
            v_done = self.tok(s_v)
            self.g_v = []

            if kind == "ffn":
                q2_guard = []
                for ci, (c0, cw) in enumerate(cfg.own_chunks):
                    def _(s, c0=c0, cw=cw, q2g=list(q2_guard)):
                        self.wt(s, *q_ev, *self.g_bias, *q2g)
                        self.inc(s.activation(self.q2c[:, 0:cw],
                                              self.qb_sb[:, c0:c0 + cw], AF.Square), s_s)
                    self.S(_)
                    q2tok = self.tok(s_s)

                    def _(t, c0=c0, cw=cw, q2tok=q2tok):
                        self.wt(t, q2tok, self.consts_sp, *self.bfree["S"])
                        inst = t.matmul(pb["S"][0:1, 0:cw], self.ones_col[:],
                                        self.q2c[:, 0:cw], start=True, stop=True)
                        self.inc(inst, s_t)
                    self.T(_)
                    tg = self.tok(s_t)
                    q2_guard = [tg]

                    def _(v, c0=c0, cw=cw, tg=tg):
                        self.wt(v, tg)
                        self.inc(v.tensor_copy(self.qn_row[:, c0:c0 + cw],
                                               pb["S"][0:1, 0:cw]), s_v)
                    self.V(_)
                    self.bfree["S"] = [self.tok(s_v)]
                k2_guard = []
                for ci, (c0, cw) in enumerate(cfg.all_chunks):
                    def _(s, c0=c0, cw=cw, k2g=list(k2_guard)):
                        self.wt(s, *k_ev, *k2g)
                        self.inc(s.activation(self.k2c[:, 0:cw],
                                              self.kb_sb[:, c0:c0 + cw], AF.Square), s_s)
                    self.S(_)
                    k2tok = self.tok(s_s)

                    def _(t, c0=c0, cw=cw, k2tok=k2tok):
                        self.wt(t, k2tok, self.consts_sp, *self.bfree["S"])
                        inst = t.matmul(pb["S"][0:1, 0:cw], self.ones_col[:],
                                        self.k2c[:, 0:cw], start=True, stop=True)
                        self.inc(inst, s_t)
                    self.T(_)
                    tg = self.tok(s_t)
                    k2_guard = [tg]

                    def _(v, c0=c0, cw=cw, tg=tg, ci=ci):
                        self.wt(v, tg, self.tok(s_v))
                        self.inc(v.tensor_reduce(self.mk2b[:], pb["S"][0:1, 0:cw],
                                                 axis=AX.X, op=ALU.max), s_v)
                        self.wt(v, self.tok(s_v))
                        if ci == 0:
                            self.inc(v.tensor_copy(self.mk2[:], self.mk2b[:]), s_v)
                        else:
                            self.inc(v.tensor_tensor(self.mk2[:], self.mk2[:],
                                                     self.mk2b[:], op=ALU.max), s_v)
                    self.V(_)
                    self.bfree["S"] = [self.tok(s_v)]

                def _(v):
                    self.wt(v, self.tok(s_v))
                    self.inc(v.tensor_scalar(self.nc_row, self.qn_row,
                                             self.mk2[:], None, op0=ALU.mult), s_v)
                self.V(_)

                def _(s):
                    self.wt(s, self.tok(s_v))
                    mtmp_row = self.rowbuf[0:1, 2 * OWN:3 * OWN]
                    self.inc(s.activation(mtmp_row, self.nc_row, AF.Ln), s_s)
                    self.wt(s, self.tok(s_s))
                    self.inc(s.activation(self.nc_row, mtmp_row, AF.Exp, scale=0.5), s_s)
                self.S(_)

                def _(v):
                    self.wt(v, self.tok(s_s))
                    self.inc(v.tensor_scalar(self.nc_row, self.nc_row, -1.0, None,
                                             op0=ALU.mult), s_v)
                self.V(_)
                ncr_tok = self.tok(s_v)

                def _(t):
                    self.wt(t, ncr_tok, *self.bfree["Y"])
                    for rt in range(OT):
                        inst = t.transpose(pb["Y"][:, rt:rt + 1],
                                           self.nc_row[:, rt * P:(rt + 1) * P],
                                           self.ones_row_f[0:1, 0:1])
                    self.inc(inst, s_t)
                self.T(_)
                tg = self.tok(s_t)

                def _(v, tg=tg):
                    self.wt(v, tg, *self.g_bias)
                    self.inc(v.tensor_copy(self.biasT[:], pb["Y"][:, 0:OT]), s_v)
                self.V(_)
                self.bfree["Y"] = [self.tok(s_v)]
                bias_tok = self.tok(s_v)
            else:
                bias_tok = None

            st_scb = [None] * OT
            st_win = [None] * OT
            st_den = [None] * OT
            ybanks = ("Y", "S")

            def stage_scores(rt):
                W = 256 * (rt + 1)
                mchunks = chunks(W)
                ech = chunks(W - 256) + [(W - 256, 256)]
                scp = self.sc_pair[rt % 2]
                for ci, (c0, cw) in enumerate(mchunks):
                    key = "ABC"[ci % 3]

                    def _(t, rt=rt, c0=c0, cw=cw, key=key):
                        self.wt(t, *q_done, *k_done, *self.bfree[key],
                                *self.g_scp[rt % 2])
                        inst = t.matmul(pb[key][:, 0:cw], q_src[:, rt * P:(rt + 1) * P],
                                        k_src[:, c0:c0 + cw], start=True, stop=True)
                        self.inc(inst, s_t)
                    self.T(_)
                    tg = self.tok(s_t)
                    if kind == "ret":
                        if ci % 2 == 0:
                            def _(s, rt=rt, c0=c0, cw=cw, key=key, tg=tg, scp=scp):
                                self.wt(s, tg)
                                self.inc(s.activation(scp[:, c0:c0 + cw],
                                                      pb[key][:, 0:cw], AF.Identity), s_s)
                            self.S(_)
                        else:
                            def _(v, rt=rt, c0=c0, cw=cw, key=key, tg=tg, scp=scp):
                                self.wt(v, tg)
                                self.inc(v.tensor_copy(scp[:, c0:c0 + cw],
                                                       pb[key][:, 0:cw]), s_v)
                            self.V(_)
                    else:
                        subs = [(e0, ew, k) for k, (e0, ew) in enumerate(ech)
                                if c0 <= e0 < c0 + cw]

                        def _(s, rt=rt, c0=c0, cw=cw, key=key, tg=tg, subs=subs, scp=scp):
                            self.wt(s, tg, bias_tok, *self.g_denp[rt % 2])
                            for (e0, ew, k) in subs:
                                self.inc(s.activation(
                                    scp[:, e0:e0 + ew],
                                    pb[key][:, e0 - c0:e0 - c0 + ew], AF.Exp,
                                    bias=self.biasT[:, rt:rt + 1],
                                    accum_out=self.den_sb[:, (rt % 2) * 8 + k:
                                                          (rt % 2) * 8 + k + 1]), s_s)
                        self.S(_)
                    self.bfree[key] = [self.tok(s_s) if kind == "ffn" or ci % 2 == 0
                                       else self.tok(s_v)]
                st_scb[rt] = (self.tok(s_s), self.tok(s_v))

                if kind == "ffn":
                    def _(v, rt=rt, W=W, scp=scp):
                        self.wt(v, *st_scb[rt], *self.g_wtp[rt % 2])
                        self.inc(v.tensor_tensor(self.wt_pair[rt % 2][:],
                                                 scp[:, W - 256:W],
                                                 self.ltri_sb[:], op=ALU.mult), s_v)
                    self.V(_)
                    st_win[rt] = self.tok(s_v)
                if kind == "ffn":
                    def _(v, rt=rt, W=W):
                        nech = len(chunks(W - 256)) + 1
                        self.wt(v, st_win[rt], *self.g_denall)
                        self.inc(v.tensor_reduce(
                            self.den_sb[:, (rt % 2) * 8 + nech - 1:(rt % 2) * 8 + nech],
                            self.wt_pair[rt % 2][:], axis=AX.X, op=ALU.add), s_v)
                        self.wt(v, self.tok(s_v))
                        self.inc(v.tensor_reduce(
                            self.denall[:, rt:rt + 1],
                            self.den_sb[:, (rt % 2) * 8:(rt % 2) * 8 + nech],
                            axis=AX.X, op=ALU.add), s_v)
                        st_den[rt] = self.tok(s_v)
                    self.V(_)

            def stage_consume(rt):
                W = 256 * (rt + 1)
                nst = W // P
                scp = self.sc_pair[rt % 2]
                ev_batch = []
                for g0 in range(0, nst, 8):
                    nk = min(8, nst - g0)
                    tkey = "T" if (g0 // 8) % 2 == 0 else "T2"

                    def _(t, rt=rt, g0=g0, nk=nk, tkey=tkey, scp=scp, nst=nst):
                        self.wt(t, *st_scb[rt], *self.bfree[tkey])
                        for k in range(nk):
                            st = g0 + k
                            inst = t.transpose(pb[tkey][:, k * P:(k + 1) * P],
                                               scp[:, st * P:(st + 1) * P],
                                               self.eye_sb[:])
                        self.inc(inst, s_t)
                    self.T(_)
                    tg = self.tok(s_t)

                    def _(v, g0=g0, nk=nk, tkey=tkey, tg=tg, nst=nst):
                        self.wt(v, tg, *self.g_scT, *self.g_yb2)
                        nw = max(0, (g0 + nk) - (nst - 2))
                        nc_ = nk - nw
                        if nc_ > 0:
                            iv = v.tensor_copy(self.scT_sb[:, g0 * P:(g0 + nc_) * P],
                                               pb[tkey][:, 0:nc_ * P])
                        if nw > 0:
                            iv = v.tensor_tensor(
                                self.scT_sb[:, (nst - 2) * P:nst * P],
                                pb[tkey][:, nc_ * P:nk * P],
                                self.ltriT_sb[:, 0:nw * P], op=ALU.mult)
                        self.inc(iv, s_v)
                    self.V(_)
                    self.bfree[tkey] = [self.tok(s_v)]
                    ev_batch.append(self.tok(s_v))
                scT_tok = self.tok(s_v)
                self.g_scp[rt % 2] = [self.tok(s_t)]
                self.g_wtp[rt % 2] = [self.tok(s_t)]

                ykey = ybanks[rt % 2]

                def _(t, rt=rt, nst=nst, ev_batch=ev_batch, ykey=ykey):
                    self.wt(t, v_done, *self.bfree[ykey])
                    for st in range(nst):
                        if st % 8 == 0:
                            self.wt(t, ev_batch[st // 8])
                        inst = t.matmul(pb[ykey][:, 0:P],
                                        self.scT_sb[:, st * P:(st + 1) * P],
                                        self.vtok[:, st * P:(st + 1) * P],
                                        start=(st == 0), stop=(st == nst - 1))
                    self.inc(inst, s_t)
                self.T(_)
                y_tok = self.tok(s_t)
                self.g_v.append(y_tok)
                self.g_scT = [y_tok]

                def _(v, rt=rt, y_tok=y_tok, ykey=ykey):
                    self.wt(v, y_tok, *self.g_ybuf)
                    self.inc(v.tensor_copy(self.ybuf[:, rt * P:(rt + 1) * P],
                                           pb[ykey][:, 0:P]), s_v)
                self.V(_)
                self.bfree[ykey] = [self.tok(s_v)]
                if kind == "ffn":
                    self.g_denp[rt % 2] = [st_den[rt]] if st_den[rt] else []

            stage_scores(0)
            for rt in range(OT):
                if rt + 1 < OT:
                    stage_scores(rt + 1)
                stage_consume(rt)
            yb_tok = self.tok(s_v)

            if kind == "ret":
                def _(s, yb_tok=yb_tok):
                    self.wt(s, yb_tok, *self.g_yb2)
                    self.inc(s.activation(self.ybuf2, self.ybuf[:], AF.Square), s_s)
                self.S(_)
                sq_tok2 = self.tok(s_s)
                yb3 = self.ybuf.ap().rearrange("p (r d) -> p r d", d=P)
                yb23 = self.scT_sb.ap().rearrange("p (r d) -> p r d", d=P)[:, 0:OT, :]

                def _(v, yb_tok=yb_tok, sq_tok2=sq_tok2):
                    self.wt(v, yb_tok, sq_tok2)
                    self.inc(v.tensor_reduce(self.gstA[:], yb3, axis=AX.X, op=ALU.add), s_v)
                    self.inc(v.tensor_reduce(self.gstB[:], yb23, axis=AX.X, op=ALU.add), s_v)
                    self.wt(v, self.tok(s_v))
                    self.inc(v.tensor_scalar(self.gstA[:], self.gstA[:], 1.0 / P, None,
                                             op0=ALU.mult), s_v)
                    self.inc(v.tensor_scalar(self.gstB[:], self.gstB[:], 1.0 / P, None,
                                             op0=ALU.mult), s_v)
                    self.wt(v, self.tok(s_v))
                    self.inc(v.tensor_tensor(self.gstC[:], self.gstA[:], self.gstA[:],
                                             op=ALU.mult), s_v)
                    self.wt(v, self.tok(s_v))
                    self.inc(v.tensor_tensor(self.gstB[:], self.gstB[:], self.gstC[:],
                                             op=ALU.subtract), s_v)
                    self.wt(v, self.tok(s_v))
                    self.inc(v.tensor_scalar(self.gstB[:], self.gstB[:], 1e-5, None,
                                             op0=ALU.add), s_v)
                self.V(_)

                def _(s):
                    self.wt(s, self.tok(s_v))
                    self.inc(s.activation(self.gstC[:], self.gstB[:], AF.Ln), s_s)
                    self.wt(s, self.tok(s_s))
                    self.inc(s.activation(self.gstB[:], self.gstC[:], AF.Exp,
                                          scale=-0.5), s_s)
                self.S(_)
                stat_tok = self.tok(s_s)

                def _(v, stat_tok=stat_tok):
                    self.wt(v, stat_tok)
                    for rt in range(OT):
                        iv = v.tensor_scalar(self.scT_sb[:, rt * P:(rt + 1) * P],
                                             self.ybuf[:, rt * P:(rt + 1) * P],
                                             self.gstA[:, rt:rt + 1],
                                             self.gstB[:, rt:rt + 1],
                                             op0=ALU.subtract, op1=ALU.mult)
                    self.inc(iv, s_v)
                self.V(_)
            else:
                def _(v, yb_tok=yb_tok):
                    self.wt(v, yb_tok, *self.g_yb2)
                    self.inc(v.reciprocal(self.gstA[:], self.denall[:]), s_v)
                    self.wt(v, self.tok(s_v))
                    for rt in range(OT):
                        iv = v.tensor_scalar(self.scT_sb[:, rt * P:(rt + 1) * P],
                                             self.ybuf[:, rt * P:(rt + 1) * P],
                                             self.gstA[:, rt:rt + 1], None,
                                             op0=ALU.mult)
                    self.inc(iv, s_v)
                self.V(_)
            yn_tok2 = self.tok(s_v)
            self.g_ybuf = [yn_tok2]
            self.g_denall = [yn_tok2]

            for g0 in range(0, OT, 8):
                nk = min(8, OT - g0)
                tkey = "T" if (g0 // 8) % 2 == 0 else "T2"

                def _(t, g0=g0, nk=nk, tkey=tkey, yn_tok2=yn_tok2):
                    self.wt(t, yn_tok2, *self.bfree[tkey])
                    for k in range(nk):
                        rt = g0 + k
                        inst = t.transpose(pb[tkey][:, k * P:(k + 1) * P],
                                           self.scT_sb[:, rt * P:(rt + 1) * P],
                                           self.eye_sb[:])
                    self.inc(inst, s_t)
                self.T(_)
                tg = self.tok(s_t)

                def _(v, g0=g0, nk=nk, tkey=tkey, tg=tg):
                    self.wt(v, tg, *self.g_yslab)
                    self.inc(v.tensor_copy(self.yslab[:, g0 * P:(g0 + nk) * P],
                                           pb[tkey][:, 0:nk * P]), s_v)
                self.V(_)
                self.bfree[tkey] = [self.tok(s_v)]
                self.g_yb2 = [self.tok(s_t)]
            slab_tok = self.tok(s_v)

            spt_box = []

            def _(s, h=h, slab_tok=slab_tok):
                self.wt(s, slab_tok)
                spt_box.append(self.dma(s, "ynsp", spill[h], self.yslab[:]))
            self.Y(_)
            self.g_yslab = [spt_box[0]]
            last_t = self.tok(s_t)
            self.g_bias = [self.tok(s_s)]
            self.g_w[buf] = [last_t]
            self.g_tab[buf] = [self.tok(s_v)]
            self.g_q = [last_t]
            self.g_k = [last_t]
        self.attn_done = (self.tok(s_t), self.tok(s_v), self.tok(s_s),
                          self.tok(self.dsem("ynsp")))

    def emit_gate_proj(self, cvv, fill_gate=(), fill_proj=()):
        fill_gate = list(fill_gate)
        fill_proj = list(fill_proj)
        cfg = self.cfg
        CT, OWN = cfg.CT, cfg.OWN
        s_t, s_v, s_s = self.s_t, self.s_v, self.s_s
        pb = self.pb
        ynl = [self.qb_sb, self.ybuf]
        g_yn = [list(self.g_q), list(self.g_ybuf) + list(self.g_yb2)]
        ya0 = self.lntmp
        wload = {}
        for dct in range(CT):
            buf = dct % 2

            def _(g, dct=dct, buf=buf):
                self.wt(g, *self.g_w[buf], *self.mem_guard)
                self.dma(g, f"wh{buf}", self.wq_h[buf].ap(),
                         self.wg_r[:, :, dct * P:(dct + 1) * P])
            self.G(_)
            wload[dct] = self.tok(self.dsem(f"wh{buf}"))
            yld_box = []

            def _(s, dct=dct, buf=buf):
                self.wt(s, *g_yn[buf], *self.g_yslab)
                yld_box.append(self.dma(s, f"ynld{buf}", ynl[buf][:], self.yn_sp_r[dct]))
            self.Y(_)
            yld = yld_box[0]

            for ci, (c0, cw) in enumerate(cfg.own_chunks):
                key = "A" if ci % 2 == 0 else "B"

                def _(t, dct=dct, buf=buf, c0=c0, cw=cw, key=key):
                    self.wt(t, wload[dct], *self.bfree[key])
                    for ct in range(CT):
                        inst = t.matmul(pb[key][:, 0:cw], self.wq_h[buf][:, ct, :],
                                        self.xn_ownT[:, ct, c0:c0 + cw],
                                        start=(ct == 0), stop=(ct == CT - 1))
                    self.inc(inst, s_t)
                self.T(_)
                tg = self.tok(s_t)
                vt = self.vtmp if ci % 2 == 0 else self.vtmp2

                def _(s, cw=cw, key=key, tg=tg, vt=vt):
                    self.wt(s, tg, *self.g_sil)
                    self.inc(s.activation(vt[:, 0:cw], pb[key][:, 0:cw], AF.Silu), s_s)
                self.S(_)
                sil = self.tok(s_s)
                self.bfree[key] = [sil]

                def _(v, dct=dct, buf=buf, c0=c0, cw=cw, sil=sil, yld=yld, vt=vt):
                    self.wt(v, sil, yld)
                    self.inc(v.tensor_scalar(ya0[:, 0:cw], ynl[buf][:, c0:c0 + cw],
                                             cvv["gnw"][:, dct:dct + 1],
                                             cvv["gnb"][:, dct:dct + 1],
                                             op0=ALU.mult, op1=ALU.add), s_v)
                    self.wt(v, self.tok(s_v))
                    self.inc(v.tensor_tensor(self.bufA[:, dct, c0:c0 + cw],
                                             vt[:, 0:cw], ya0[:, 0:cw], op=ALU.mult), s_v)
                self.V(_)
                self.g_sil = [self.tok(s_v)]
            self.g_w[buf] = [tg]
            g_yn[buf] = [self.tok(s_v)]
            npump = (len(fill_gate) + CT - 1 - dct) // (CT - dct)
            for _i in range(min(npump, len(fill_gate))):
                fill_gate.pop(0)()
        for fn in fill_gate:
            fn()
        og_tok = self.tok(s_v)
        self.g_lntmp = [og_tok]
        self.dbg_dump("og", self.bufA, [P, CT * OWN], [og_tok])

        sslb = [(self.kb_sb, 0), (self.kb_sb, OWN)]
        g_ssl = [[], []]
        for ct2 in range(CT):
            buf = ct2 % 2

            def _(g, ct2=ct2, buf=buf):
                self.wt(g, *self.g_w2[buf])
                self.dma(g, f"wk{buf}", self.wk_h[buf].ap(),
                         self.wproj_r[:, :, ct2 * P:(ct2 + 1) * P])
            self.G(_)
            wl = self.tok(self.dsem(f"wk{buf}"))
            for ci, (c0, cw) in enumerate(cfg.own_chunks):
                key = "A" if ci % 2 == 0 else "B"

                def _(t, ct2=ct2, buf=buf, c0=c0, cw=cw, key=key, wl=wl):
                    self.wt(t, wl, og_tok, *self.bfree[key])
                    for dct in range(CT):
                        inst = t.matmul(pb[key][:, 0:cw], self.wk_h[buf][:, dct, :],
                                        self.bufA[:, dct, c0:c0 + cw],
                                        start=(dct == 0), stop=(dct == CT - 1))
                    self.inc(inst, s_t)
                self.T(_)
                tg = self.tok(s_t)

                def _(s, buf=buf, c0=c0, cw=cw, key=key, tg=tg):
                    self.wt(s, tg, *g_ssl[buf])
                    tb, off = sslb[buf]
                    self.inc(s.activation(tb[:, off + c0:off + c0 + cw], pb[key][:, 0:cw],
                                          AF.Identity), s_s)
                self.S(_)
                self.bfree[key] = [self.tok(s_s)]
            sev = self.tok(s_s)

            def _(s, ct2=ct2, buf=buf, sev=sev):
                self.wt(s, sev)
                tb, off = sslb[buf]
                self.dma(s, f"ssp{buf}", self.s_sp_r[ct2], tb[:, off:off + OWN])
            self.Y(_)
            g_ssl[buf] = [self.tok(self.dsem(f"ssp{buf}"))]
            self.g_w2[buf] = [tg]
            npump = (len(fill_proj) + CT - 1 - ct2) // (CT - ct2)
            for _i in range(min(npump, len(fill_proj))):
                fill_proj.pop(0)()
        for fn in fill_proj:
            fn()
        self.proj_tok = self.tok(s_t)
        self.proj_sqd = (self.tok(self.dsem("ssp0")), self.tok(self.dsem("ssp1")))
        self.g_ybuf = [self.tok(s_v)]
        self.g_yb2 = []
        self.g_q = [self.tok(s_v)]
        self.g_k = [*self.proj_sqd]
        self.g_v = [self.tok(s_v)]

    def emit_fmajor_ln(self, src3, dst3, total_width, svec, bvec, tag, ready=()):
        tok = None
        for col0 in range(0, total_width, self.cfg.OWN):
            stats, tail = self.ln_units(src3, dst3, col0,
                                        min(self.cfg.OWN, total_width - col0),
                                        svec, bvec, ready)
            for fn in stats + tail:
                fn()
            tok = self._ln_tok
        return tok

    def ln_units(self, src3, dst3, col0, width, svec, bvec, ready,
                 scr_pair=None, apply_ready=None, rowp=0):
        cfg = self.cfg
        C, CT = cfg.C, cfg.CT
        s_t, s_v, s_s = self.s_t, self.s_v, self.s_s
        pb = self.pb
        OWN = cfg.OWN
        wch = [(col0 + o, w) for (o, w) in chunks(width)]
        srow0, qrow0, mrow0 = 0, OWN, 2 * OWN
        rb = self.rowbuf
        scr = scr_pair or (self.vtmp, self.vtmp2)
        if not hasattr(self, "g_sq_pair"):
            self.g_sq_pair = [list(self.g_sqtmp), list(self.g_sqtmp)]
        sq_guard = self.g_sq_pair
        stats_units = []
        ev_toks = {}

        def stat_unit(ci, c0, cw, ct):
            sqb = scr[ct % 2]

            def _(s, c0=c0, cw=cw, ct=ct, sqb=sqb):
                self.wt(s, *ready, *sq_guard[ct % 2])
                self.inc(s.activation(sqb[:, 0:cw], src3[:, ct, c0:c0 + cw],
                                      AF.Square), s_s)
            self.S(_)
            sqk = self.tok(s_s)

            def _(t, c0=c0, cw=cw, ct=ct, sqk=sqk, sqb=sqb):
                self.wt(t, sqk, *ready, self.consts_sp)
                if ct == 0:
                    self.wt(t, *self.bfree["S"], *self.bfree["C"])
                t.matmul(pb["S"][0:1, 0:cw], self.ones_col[:],
                         src3[:, ct, c0:c0 + cw],
                         start=(ct == 0), stop=(ct == CT - 1))
                inst = t.matmul(pb["C"][0:1, 0:cw], self.ones_col[:],
                                sqb[:, 0:cw],
                                start=(ct == 0), stop=(ct == CT - 1))
                self.inc(inst, s_t)
            self.T(_)
            sq_guard[ct % 2] = [self.tok(s_t)]
            if ct == CT - 1:
                tg = self.tok(s_t)

                def _(v, c0=c0, cw=cw, tg=tg):
                    self.wt(v, tg, self.tok(s_v))
                    self.inc(v.tensor_copy(
                        rb[rowp:rowp + 1, srow0 + c0 - col0:srow0 + c0 - col0 + cw],
                        pb["S"][0:1, 0:cw]), s_v)
                    self.inc(v.tensor_copy(
                        rb[rowp:rowp + 1, qrow0 + c0 - col0:qrow0 + c0 - col0 + cw],
                        pb["C"][0:1, 0:cw]), s_v)
                self.V(_)
                self.bfree["S"] = [self.tok(s_v)]
                self.bfree["C"] = [self.tok(s_v)]
                ev_toks[ci] = self.tok(s_v)

        for ci, (c0, cw) in enumerate(wch):
            for ct in range(CT):
                stats_units.append(lambda ci=ci, c0=c0, cw=cw, ct=ct:
                                   stat_unit(ci, c0, cw, ct))

        srow = rb[rowp:rowp + 1, srow0:srow0 + width]
        qrow = rb[rowp:rowp + 1, qrow0:qrow0 + width]
        mrow = rb[rowp:rowp + 1, mrow0:mrow0 + width]
        rows_box = {}
        tail_units = []

        def rows_unit():
            def _(v):
                self.wt(v, self.tok(s_v))
                self.inc(v.tensor_scalar(srow, srow, 1.0 / C, None, op0=ALU.mult), s_v)
                self.inc(v.tensor_scalar(qrow, qrow, 1.0 / C, None, op0=ALU.mult), s_v)
                self.wt(v, self.tok(s_v))
                self.inc(v.tensor_tensor(mrow, srow, srow, op=ALU.mult), s_v)
                self.wt(v, self.tok(s_v))
                self.inc(v.tensor_tensor(qrow, qrow, mrow, op=ALU.subtract), s_v)
                self.wt(v, self.tok(s_v))
                self.inc(v.tensor_scalar(qrow, qrow, 1e-5, None, op0=ALU.add), s_v)
            self.V(_)

            def _(s):
                self.wt(s, self.tok(s_v))
                self.inc(s.activation(mrow, qrow, AF.Ln), s_s)
                self.wt(s, self.tok(s_s))
                self.inc(s.activation(qrow, mrow, AF.Exp, scale=-0.5), s_s)
            self.S(_)
            rows_box["tok"] = self.tok(s_s)
        tail_units.append(rows_unit)

        def bcast_unit(ci, c0, cw):
            def _(t, c0=c0, cw=cw):
                self.wt(t, rows_box["tok"], *self.bfree["D"], *self.bfree["Y"],
                        self.consts_sp)
                inst = t.matmul(pb["D"][:, 0:cw], self.ones_row_f[rowp:rowp + 1, :],
                                rb[rowp:rowp + 1, srow0 + c0 - col0:srow0 + c0 - col0 + cw],
                                start=True, stop=True)
                inst = t.matmul(pb["Y"][:, 0:cw], self.ones_row_f[rowp:rowp + 1, :],
                                rb[rowp:rowp + 1, qrow0 + c0 - col0:qrow0 + c0 - col0 + cw],
                                start=True, stop=True)
                self.inc(inst, s_t)
            self.T(_)
            rows_box[("bc", ci)] = self.tok(s_t)

        def apply_unit(ci, c0, cw, ct):
            tg = rows_box[("bc", ci)]

            def _(v, c0=c0, cw=cw, ct=ct, tg=tg):
                self.wt(v, tg, *self.g_lntmp, *(apply_ready or []))
                self.inc(v.tensor_tensor(self.lntmp[:, 0:cw], src3[:, ct, c0:c0 + cw],
                                         pb["D"][:, 0:cw], op=ALU.subtract), s_v)
                self.wt(v, self.tok(s_v))
                self.inc(v.tensor_tensor(self.lntmp[:, 0:cw], self.lntmp[:, 0:cw],
                                         pb["Y"][:, 0:cw], op=ALU.mult), s_v)
                self.wt(v, self.tok(s_v))
                self.inc(v.tensor_scalar(dst3[:, ct, c0:c0 + cw], self.lntmp[:, 0:cw],
                                         svec[:, ct:ct + 1], bvec[:, ct:ct + 1],
                                         op0=ALU.mult, op1=ALU.add), s_v)
                self.g_lntmp = [self.tok(s_v)]
            self.V(_)
            if ct == CT - 1:
                self.bfree["D"] = [self.tok(s_v)]
                self.bfree["Y"] = [self.tok(s_v)]
                self._ln_tok = self.tok(s_v)

        for ci, (c0, cw) in enumerate(wch):
            tail_units.append(lambda ci=ci, c0=c0, cw=cw: bcast_unit(ci, c0, cw))
            for ct in range(CT):
                tail_units.append(lambda ci=ci, c0=c0, cw=cw, ct=ct:
                                  apply_unit(ci, c0, cw, ct))
        return stats_units, tail_units

    def emit_wo_x2_ln2(self, cvv):
        cfg = self.cfg
        CT, OWN, H = cfg.CT, cfg.OWN, cfg.H
        s_t, s_v, s_s = self.s_t, self.s_v, self.s_s
        pb = self.pb
        attn_t = self.attn_done[0]
        ctx_toks = []
        for h in range(H):
            def _(s, h=h):
                self.wt(s, attn_t, self.attn_done[3])
                ctx_toks.append(self.dma(s, f"ctxld{h % 4}",
                                         self.xn_allT[:, h, 0:OWN], self.ctx_sp_r[h]))
            self.Y(_)
        ctx_ld_all = [self.tok(self.dsem(f"ctxld{j}")) for j in range(min(4, H))]

        x2l = [self.qb_sb, self.ybuf]
        g_x2 = [[attn_t], [attn_t]]
        nch = len(cfg.own_chunks)
        sum_banks = ["S", "D"]
        sq_banks = ["C", "Y"]
        for ct2 in range(CT):
            buf = ct2 % 2

            def _(g, ct2=ct2, buf=buf):
                self.wt(g, *self.g_w[buf])
                self.dma(g, f"wh{buf}", self.wq_h[buf].ap(),
                         self.woT_r[:, :, ct2 * P:(ct2 + 1) * P])
            self.G(_)
            wl = self.tok(self.dsem(f"wh{buf}"))
            for ci, (c0, cw) in enumerate(cfg.own_chunks):
                key = "A" if ci % 2 == 0 else "B"

                def _(t, ct2=ct2, buf=buf, c0=c0, cw=cw, key=key, wl=wl):
                    self.wt(t, wl, *ctx_ld_all, *self.bfree[key])
                    for hh in range(CT):
                        inst = t.matmul(pb[key][:, 0:cw], self.wq_h[buf][:, hh, :],
                                        self.xn_allT[:, hh, c0:c0 + cw],
                                        start=(hh == 0), stop=(hh == CT - 1))
                    self.inc(inst, s_t)
                self.T(_)
                tg = self.tok(s_t)

                def _(v, ct2=ct2, buf=buf, c0=c0, cw=cw, key=key, tg=tg):
                    self.wt(v, tg, *g_x2[buf])
                    self.inc(v.tensor_tensor(x2l[buf][:, c0:c0 + cw], pb[key][:, 0:cw],
                                             self.xn_ownT[:, ct2, c0:c0 + cw],
                                             op=ALU.add), s_v)
                self.V(_)
                x2tok = self.tok(s_v)
                self.bfree[key] = [x2tok]

                def _(s, buf=buf, c0=c0, cw=cw, x2tok=x2tok):
                    self.wt(s, x2tok, *self.g_sqtmp)
                    self.inc(s.activation(self.sqtmp[:, 0:cw], x2l[buf][:, c0:c0 + cw],
                                          AF.Square), s_s)
                self.S(_)
                sqk = self.tok(s_s)

                def _(t, ct2=ct2, buf=buf, c0=c0, cw=cw, ci=ci, sqk=sqk):
                    self.wt(t, sqk, self.consts_sp)
                    if ct2 == 0:
                        self.wt(t, *self.bfree[sum_banks[ci]], *self.bfree[sq_banks[ci]])
                    t.matmul(pb[sum_banks[ci]][0:1, 0:cw], self.ones_col[:],
                             x2l[buf][:, c0:c0 + cw],
                             start=(ct2 == 0), stop=(ct2 == CT - 1))
                    inst = t.matmul(pb[sq_banks[ci]][0:1, 0:cw], self.ones_col[:],
                                    self.sqtmp[:, 0:cw],
                                    start=(ct2 == 0), stop=(ct2 == CT - 1))
                    self.inc(inst, s_t)
                self.T(_)
                self.g_sqtmp = [self.tok(s_t)]
            stat_t = self.tok(s_t)

            def _(s, ct2=ct2, buf=buf, stat_t=stat_t):
                self.wt(s, stat_t)
                self.dma(s, f"x2sp{buf}", self.x2_sp_r[ct2], x2l[buf][:])
            self.Y(_)
            g_x2[buf] = [self.tok(self.dsem(f"x2sp{buf}"))]
            self.g_w[buf] = [stat_t]
        stats_done = self.tok(s_t)

        width = OWN
        rb = self.rowbuf
        for ci, (c0, cw) in enumerate(cfg.own_chunks):
            def _(v, ci=ci, c0=c0, cw=cw):
                self.wt(v, stats_done)
                self.inc(v.tensor_copy(rb[0:1, c0:c0 + cw],
                                       pb[sum_banks[ci]][0:1, 0:cw]), s_v)
                self.inc(v.tensor_copy(rb[0:1, OWN + c0:OWN + c0 + cw],
                                       pb[sq_banks[ci]][0:1, 0:cw]), s_v)
            self.V(_)
            self.bfree[sum_banks[ci]] = [self.tok(s_v)]
            self.bfree[sq_banks[ci]] = [self.tok(s_v)]
        srow = rb[0:1, 0:width]
        qrow = rb[0:1, OWN:OWN + width]
        mrow = rb[0:1, 2 * OWN:2 * OWN + width]

        def _(v):
            self.wt(v, self.tok(s_v))
            self.inc(v.tensor_scalar(srow, srow, 1.0 / cfg.C, None, op0=ALU.mult), s_v)
            self.inc(v.tensor_scalar(qrow, qrow, 1.0 / cfg.C, None, op0=ALU.mult), s_v)
            self.wt(v, self.tok(s_v))
            self.inc(v.tensor_tensor(mrow, srow, srow, op=ALU.mult), s_v)
            self.wt(v, self.tok(s_v))
            self.inc(v.tensor_tensor(qrow, qrow, mrow, op=ALU.subtract), s_v)
            self.wt(v, self.tok(s_v))
            self.inc(v.tensor_scalar(qrow, qrow, 1e-5, None, op0=ALU.add), s_v)
        self.V(_)

        def _(s):
            self.wt(s, self.tok(s_v))
            self.inc(s.activation(mrow, qrow, AF.Ln), s_s)
            self.wt(s, self.tok(s_s))
            self.inc(s.activation(qrow, mrow, AF.Exp, scale=-0.5), s_s)
        self.S(_)
        rows_tok = self.tok(s_s)

        bcast_done = {}
        for ci, (c0, cw) in enumerate(cfg.own_chunks):
            def _(t, ci=ci, c0=c0, cw=cw):
                self.wt(t, rows_tok, *self.bfree[sum_banks[ci]], *self.bfree[sq_banks[ci]])
                t.matmul(pb[sum_banks[ci]][:, 0:cw], self.ones_row_f[0:1, :],
                         srow[:, c0:c0 + cw], start=True, stop=True)
                inst = t.matmul(pb[sq_banks[ci]][:, 0:cw], self.ones_row_f[0:1, :],
                                qrow[:, c0:c0 + cw], start=True, stop=True)
                self.inc(inst, s_t)
            self.T(_)
            bcast_done[ci] = self.tok(s_t)
        for ct2 in range(CT):
            buf = ct2 % 2

            xld_box = []

            def _(s, ct2=ct2, buf=buf):
                self.wt(s, *g_x2[buf])
                xld_box.append(self.dma(s, f"x2ld{buf}", x2l[buf][:], self.x2_sp_r[ct2]))
            self.Y(_)
            xld = xld_box[0]
            for ci, (c0, cw) in enumerate(cfg.own_chunks):
                def _(v, ct2=ct2, buf=buf, ci=ci, c0=c0, cw=cw, xld=xld):
                    self.wt(v, xld, bcast_done[ci], *self.g_lntmp, attn_t)
                    self.inc(v.tensor_tensor(self.lntmp[:, 0:cw], x2l[buf][:, c0:c0 + cw],
                                             pb[sum_banks[ci]][:, 0:cw], op=ALU.subtract), s_v)
                    self.wt(v, self.tok(s_v))
                    self.inc(v.tensor_tensor(self.lntmp[:, 0:cw], self.lntmp[:, 0:cw],
                                             pb[sq_banks[ci]][:, 0:cw], op=ALU.mult), s_v)
                    self.wt(v, self.tok(s_v))
                    self.inc(v.tensor_scalar(self.bufA[:, ct2, c0:c0 + cw],
                                             self.lntmp[:, 0:cw],
                                             cvv["ln2s"][:, ct2:ct2 + 1],
                                             cvv["ln2b"][:, ct2:ct2 + 1],
                                             op0=ALU.mult, op1=ALU.add), s_v)
                    self.g_lntmp = [self.tok(s_v)]
                self.V(_)
            g_x2[buf] = [self.tok(s_v)]
        for ci in range(len(cfg.own_chunks)):
            self.bfree[sum_banks[ci]] = [self.tok(s_v)]
            self.bfree[sq_banks[ci]] = [self.tok(s_v)]
        self.x2n_tok = self.tok(s_v)
        self.g_x2l = g_x2
        self.dbg_dump("x2nT", self.bufA, [P, CT * OWN], [self.x2n_tok])

    def emit_ff_stageA(self, hg_sp_w):
        cfg = self.cfg
        CT, FT = cfg.CT, cfg.FT
        s_t, s_s = self.s_t, self.s_s
        pb = self.pb
        hgt = [self.qb_sb, self.ybuf]
        g_hg = [list(self.g_x2l[0]), list(self.g_x2l[1])]
        for ft in range(FT):
            buf = ft % 2

            def _(g, ft=ft, buf=buf):
                self.wt(g, *self.g_w[buf], self.pc["ff1"])
                self.dma(g, f"wh{buf}", self.wq_h[buf].ap(),
                         self.ff1Tbf_r[:, :, ft * P:(ft + 1) * P])
            self.G(_)
            wl = self.tok(self.dsem(f"wh{buf}"))
            for ci, (c0, cw) in enumerate(cfg.own_chunks):
                key = "A" if ci % 2 == 0 else "B"

                def _(t, ft=ft, buf=buf, c0=c0, cw=cw, key=key, wl=wl):
                    self.wt(t, wl, self.x2n_tok, *self.bfree[key])
                    for ct in range(CT):
                        inst = t.matmul(pb[key][:, 0:cw], self.wq_h[buf][:, ct, :],
                                        self.bufA[:, ct, c0:c0 + cw],
                                        start=(ct == 0), stop=(ct == CT - 1))
                    self.inc(inst, s_t)
                self.T(_)
                tg = self.tok(s_t)

                def _(s, buf=buf, c0=c0, cw=cw, key=key, tg=tg):
                    self.wt(s, tg, *g_hg[buf])
                    self.inc(s.activation(hgt[buf][:, c0:c0 + cw], pb[key][:, 0:cw],
                                          AF.Gelu_apprx_tanh), s_s)
                self.S(_)
                self.bfree[key] = [self.tok(s_s)]
                self.g_w[buf] = [tg]
            hev = self.tok(s_s)

            def _(s, ft=ft, buf=buf, hev=hev):
                self.wt(s, hev)
                self.dma(s, f"hgsp{buf}", hg_sp_w[ft], hgt[buf][:])
            self.Y(_)
            g_hg[buf] = [self.tok(self.dsem(f"hgsp{buf}"))]
        self.stageA_t = self.tok(s_t)
        self.stageA_sqd = (self.tok(self.dsem("hgsp0")), self.tok(self.dsem("hgsp1")))

    def emit_stageB_combine(self, hg_sp_r, cvv, xown_r, outown_r):
        cfg = self.cfg
        CT, OWN, FT, C, R = cfg.CT, cfg.OWN, cfg.FT, cfg.C, cfg.R
        s_t, s_v, s_s = self.s_t, self.s_v, self.s_s
        pb = self.pb
        sb_es = ExitStack()
        wff2 = [self.sbuf(f"wff2_{i}", [P, FT, P], BF16, es=sb_es) for i in range(2)]
        sgch = self.sbuf("sgch", [P, CT, 512], BF16, es=sb_es)
        sgT = self.sbuf("sgT", [P, C], BF16, es=sb_es)
        cb_t0 = self.sbuf("cb_t0", [P, 512], BF16, es=sb_es)
        cb_t1 = self.sbuf("cb_t1", [P, 512], BF16, es=sb_es)
        cb_t2 = self.sbuf("cb_t2", [P, 512], BF16, es=sb_es)
        cb_g = self.sbuf("cb_g", [P, 512], BF16, es=sb_es)
        cb_gs = self.sbuf("cb_gs", [P, 512], BF16, es=sb_es)
        x2sl = [self.sbuf(f"x2sl{i}", [P, 512], BF16, es=sb_es) for i in range(2)]
        ssl = [self.sbuf(f"ssl{i}", [P, 512], BF16, es=sb_es) for i in range(2)]
        mrows = self.sbuf("mrows", [P, 3 * 512], BF16, es=sb_es)
        mb = [(mrows, 0), (mrows, 512), (mrows, 1024)]
        xa2 = self.sbuf("xa2", [P, C], F32, es=sb_es)
        outb1 = self.sbuf("outb1", [P, C], F32, es=sb_es)
        wupT_sb = self.sbuf("wupT_sb", [R, C], BF16, es=sb_es)
        self.wupT_sb = wupT_sb
        self.sb_es = sb_es

        def _(g):
            self.wt(g, *self.mem_guard)
            self.dma(g, "gconsts", wupT_sb[:], self.w_upT_d[:])
        self.G(_)
        wup_tok = self.tok(self.dsem("gconsts"))
        hgflat = self.xn_allT.ap().rearrange("p a n -> p (a n)")
        g_wff2 = [[], []]
        g_ssl = [[self.proj_sqd[0]], [self.proj_sqd[1]]]
        g_x2sl = [[], []]
        g_sgch = []
        g_cb = []
        g_sgT = []
        g_out = [[], []]
        g_hgbuf = [self.stageA_t, self.attn_done[0]]

        sg_dbg = None
        if "sg" in self.dbg:
            sg_dbg = self.nc.dram_tensor("dbg_sg", [OWN, C], F32,
                                         kind="ExternalOutput").ap() \
                .rearrange("(rt p) c -> rt p c", p=P)
            self.out_names.append("dbg_sg")
        for ci, (c0, cw) in enumerate(cfg.own_chunks):
            hg3 = hgflat[:, 0:FT * cw].rearrange("p (a n) -> p a n", n=cw)

            hg_box = []

            def _(s, ci=ci, c0=c0, cw=cw, hg3=hg3):
                self.wt(s, *self.stageA_sqd, *g_hgbuf, *self.mem_guard)
                hg_box.append(self.dma(s, "hgld", hg3, hg_sp_r[:, :, c0:c0 + cw]))
            self.Y(_)
            hg_ld = hg_box[0]

            mk_toks = []
            for j in range(2):
                def _(t, j=j, ci=ci, c0=c0, cw=cw):
                    self.wt(t, self.masks_tok, self.consts_sp, *self.bfree["D"])
                    inst = t.matmul(pb["D"][:, 0:cw], self.ones_row_bf[:],
                                    self.masks_bf[0:1, j * OWN + c0:j * OWN + c0 + cw],
                                    start=True, stop=True)
                    self.inc(inst, s_t)
                self.T(_)
                tg = self.tok(s_t)

                def _(v, cw=cw, tg=tg, j=j):
                    self.wt(v, tg)
                    tb, off = mb[j]
                    self.inc(v.tensor_copy(tb[:, off:off + cw], pb["D"][:, 0:cw]), s_v)
                self.V(_)
                self.bfree["D"] = [self.tok(s_v)]
                mk_toks.append(self.tok(s_v))
            def _(v, cw=cw):
                self.wt(v, *mk_toks)
                self.inc(v.tensor_tensor(mb[2][0][:, mb[2][1]:mb[2][1] + cw],
                                         mb[0][0][:, mb[0][1]:mb[0][1] + cw],
                                         mb[1][0][:, mb[1][1]:mb[1][1] + cw],
                                         op=ALU.add), s_v)
                self.wt(v, self.tok(s_v))
                self.inc(v.tensor_scalar(mb[2][0][:, mb[2][1]:mb[2][1] + cw],
                                         mb[2][0][:, mb[2][1]:mb[2][1] + cw],
                                         -1.0, 1.0, op0=ALU.mult, op1=ALU.add), s_v)
            self.V(_)
            mk_toks.append(self.tok(s_v))

            for ct2 in range(CT):
                buf = ct2 % 2

                wl_q = []

                def _(g, ct2=ct2, buf=buf):
                    self.wt(g, *g_wff2[buf], *self.mem_guard, self.pc["ff2"])
                    wl_q.append(self.dma(g, f"wf2{buf}", wff2[buf][:, :, :],
                                         self.ff2Tbf_r[:, :, ct2 * P:(ct2 + 1) * P]))
                self.G(_)

                sl_box = []

                def _(s, ct2=ct2, buf=buf, c0=c0, cw=cw):
                    self.wt(s, *g_ssl[buf], *self.mem_guard)
                    sl_box.append(self.dma(s, f"sld{buf}", ssl[buf][:, 0:cw],
                                           self.s_sp_r[ct2][:, c0:c0 + cw]))
                    self.wt(s, *g_x2sl[buf])
                    sl_box.append(self.dma(s, f"x2d{buf}", x2sl[buf][:, 0:cw],
                                           self.x2_sp_r[ct2][:, c0:c0 + cw]))
                self.Y(_)
                sl_ld = sl_box

                key = "A" if ct2 % 2 == 0 else "B"

                def _(t, ct2=ct2, buf=buf, c0=c0, cw=cw, key=key, wl_q=wl_q,
                      hg_ld=hg_ld, hg3=hg3):
                    self.wt(t, hg_ld, *self.bfree[key], wl_q[0])
                    for ft in range(FT):
                        inst = t.matmul(pb[key][:, 0:cw], wff2[buf][:, ft, :],
                                        hg3[:, ft, 0:cw], start=(ft == 0),
                                        stop=(ft == FT - 1))
                    self.inc(inst, s_t)
                    self.wt(t, *self.bfree["S"], wup_tok)
                    inst = t.matmul(pb["S"][:, 0:cw],
                                    self.wupT_sb[0:R, ct2 * P:(ct2 + 1) * P],
                                    self.lr_sb[0:R, c0:c0 + cw], start=True, stop=True)
                    self.inc(inst, s_t)
                self.T(_)
                ff_t = self.tok(s_t)
                g_wff2[buf] = [ff_t]

                def _(s, ct2=ct2, c0=c0, cw=cw):
                    self.wt(s, *g_cb)
                    self.inc(s.activation(cb_t0[:, 0:cw], self.xn_ownT[:, ct2, c0:c0 + cw],
                                          AF.Identity,
                                          bias=cvv["sbias"][:, ct2:ct2 + 1]), s_s)
                self.S(_)
                t0_tok = self.tok(s_s)

                def _(v, ct2=ct2, buf=buf, c0=c0, cw=cw, key=key, ff_t=ff_t,
                      sl_ld=sl_ld, t0_tok=t0_tok):
                    self.wt(v, ff_t, *sl_ld, t0_tok, *mk_toks, *g_cb)
                    self.inc(v.tensor_tensor(cb_t1[:, 0:cw],
                                             self.xn_ownT[:, ct2, c0:c0 + cw],
                                             pb["S"][:, 0:cw], op=ALU.add), s_v)
                    self.inc(v.tensor_tensor(cb_t2[:, 0:cw], x2sl[buf][:, 0:cw],
                                             pb[key][:, 0:cw], op=ALU.add), s_v)
                    self.wt(v, self.tok(s_v))
                    self.inc(v.tensor_tensor(cb_g[:, 0:cw], cb_t0[:, 0:cw],
                                             mb[0][0][:, mb[0][1]:mb[0][1] + cw],
                                             op=ALU.mult), s_v)
                    self.inc(v.tensor_tensor(cb_gs[:, 0:cw], cb_t1[:, 0:cw],
                                             mb[1][0][:, mb[1][1]:mb[1][1] + cw],
                                             op=ALU.mult), s_v)
                    self.wt(v, self.tok(s_v))
                    self.inc(v.tensor_tensor(cb_t0[:, 0:cw], cb_t2[:, 0:cw],
                                             mb[2][0][:, mb[2][1]:mb[2][1] + cw],
                                             op=ALU.mult), s_v)
                    self.inc(v.tensor_tensor(cb_g[:, 0:cw], cb_g[:, 0:cw],
                                             cb_gs[:, 0:cw], op=ALU.add), s_v)
                    self.wt(v, self.tok(s_v))
                    self.inc(v.tensor_tensor(cb_g[:, 0:cw], cb_g[:, 0:cw],
                                             cb_t0[:, 0:cw], op=ALU.add), s_v)
                self.V(_)
                gv = self.tok(s_v)
                self.bfree[key] = [gv]
                self.bfree["S"] = [gv]
                g_x2sl[buf] = [gv]

                def _(s, gv=gv, cw=cw):
                    self.wt(s, gv)
                    self.inc(s.activation(cb_t1[:, 0:cw], cb_g[:, 0:cw], AF.Silu), s_s)
                self.S(_)
                gs_tok = self.tok(s_s)

                def _(v, ct2=ct2, buf=buf, cw=cw, gs_tok=gs_tok):
                    self.wt(v, gs_tok, *g_sgch)
                    self.inc(v.tensor_tensor(sgch[:, ct2, 0:cw], cb_t1[:, 0:cw],
                                             ssl[buf][:, 0:cw], op=ALU.mult), s_v)
                self.V(_)
                g_cb = [self.tok(s_v)]
                g_ssl[buf] = [self.tok(s_v)]
            sg_tok = self.tok(s_v)
            g_hgbuf = [self.tok(s_t)]

            for rl in range(cw // P):
                rt = (c0 + rl * P) // P
                ob = outb1
                for g0 in range(0, CT, 8):
                    nk = min(8, CT - g0)
                    tkey = "T" if (g0 // 8) % 2 == 0 else "T2"

                    def _(t, rl=rl, g0=g0, nk=nk, tkey=tkey, sg_tok=sg_tok):
                        self.wt(t, sg_tok, *self.bfree[tkey])
                        for k in range(nk):
                            ct2 = g0 + k
                            inst = t.transpose(pb[tkey][:, k * P:(k + 1) * P],
                                               sgch[:, ct2, rl * P:(rl + 1) * P],
                                               self.eye_sb[:])
                        self.inc(inst, s_t)
                    self.T(_)
                    tg = self.tok(s_t)

                    def _(v, g0=g0, nk=nk, tkey=tkey, tg=tg, rl=rl):
                        self.wt(v, tg, *(g_out[0]), *g_sgT)
                        self.inc(v.tensor_copy(sgT[:, g0 * P:(g0 + nk) * P],
                                               pb[tkey][:, 0:nk * P]), s_v)
                    self.V(_)
                    self.bfree[tkey] = [self.tok(s_v)]
                tp_tok = self.tok(s_v)
                g_sgch = [tp_tok]
                if sg_dbg is not None:
                    def _(g, rt=rt, tp_tok=tp_tok):
                        self.wt(g, tp_tok)
                        self.dbg_tok = self.dma(g, "dbg", sg_dbg[rt], sgT[:])
                    self.G(_)
                    self.dbg_guard = [self.dbg_tok]

                xo_box = []

                def _(s, rt=rt, rl=rl):
                    self.wt(s, *(g_out[0]), *self.mem_guard)
                    xo_box.append(self.dma(s, "xo", xa2[:], xown_r[rt]))
                self.Y(_)
                xld = xo_box[0]

                def _(v, rl=rl, tp_tok=tp_tok, xld=xld, ob=ob):
                    self.wt(v, tp_tok, xld, *(g_out[0]))
                    self.inc(v.tensor_tensor(ob[:], xa2[:], sgT[:],
                                             op=ALU.add), s_v)
                self.V(_)
                ov = self.tok(s_v)
                g_sgT = [ov]

                def _(s, rt=rt, ov=ov, ob=ob, rl=rl):
                    self.wt(s, ov)
                    self.dma(s, "out", outown_r[rt], ob[:])
                self.Y(_)
                g_out[0] = [self.tok(self.dsem("out")), ov]



def build_program(cfg, dbg=()):
    b = Builder(cfg, dbg)
    b.build()
    return b


def host_inputs(cfg, inputs, half):
    f32c = lambda a: np.ascontiguousarray(np.asarray(a, np.float32))
    wq = f32c(inputs["L0_ffn_wq"]); wk = f32c(inputs["L0_ffn_wk"])
    wv = f32c(inputs["L0_ffn_wv"]); wo = f32c(inputs["L0_ffn_wo"])
    m = {
        "w_qkv": f32c(inputs["L0_attn_w_qkv"]),
        "w_g": f32c(inputs["L0_attn_w_g"]),
        "w_proj": f32c(inputs["L0_attn_w_proj"]),
        "wqkvT_f": np.ascontiguousarray(
            np.concatenate([wq.T, wk.T, wv.T], axis=1)),
        "woT": np.ascontiguousarray(wo.T),
        "ff1T": np.ascontiguousarray(f32c(inputs["L0_ffn_w_ff1"]).T),
        "ff2T": np.ascontiguousarray(f32c(inputs["L0_ffn_w_ff2"]).T),
        "w_downT": np.ascontiguousarray(f32c(inputs["L0_ffn_w_down"]).T),
        "w_upT": np.ascontiguousarray(f32c(inputs["L0_ffn_w_up"]).T),
        "wrT": np.ascontiguousarray(f32c(inputs["L0_ffn_w_router"]).T),
        "cvecs": np.concatenate(
            [cvec(inputs["L0_attn_gn_weight"], cfg),
             cvec(inputs["L0_attn_gn_bias"], cfg),
             cvec(inputs["L0_ffn_ln1_scale"], cfg),
             cvec(inputs["L0_ffn_ln1_bias"], cfg),
             cvec(inputs["L0_ffn_ln2_scale"], cfg),
             cvec(inputs["L0_ffn_ln2_bias"], cfg),
             cvec(inputs["L0_ffn_simple_bias"], cfg)], axis=1),
    }
    cq, sq, ck, sk = xpos_tables(cfg, half)
    ltri, ltriT, eye, pmatT = host_consts(cfg, half)
    m.update(cq=cq, sq=sq, ck=ck, sk=sk, ltri=ltri, ltriT=ltriT, eye=eye,
             pmatT=pmatT)
    return m


def core_inputs(cfg, inputs, shared, core):
    b, half = core // 2, core % 2
    x = np.asarray(inputs["L0_attn_x"], np.float32)[b]
    m = dict(shared[half])
    m["x_all"] = np.ascontiguousarray(x)
    m["x_own"] = np.ascontiguousarray(x[half::2, :])
    return m


def kernel(**inputs):
    cfg = CFG()
    bld = build_program(cfg)
    shared = [host_inputs(cfg, inputs, half) for half in range(2)]
    in_maps = [core_inputs(cfg, inputs, shared, c) for c in range(8)]
    res = run_bass_kernel_spmd(bld.nc, in_maps, core_ids=list(range(8)))
    out = np.empty((cfg.NB, cfg.T, cfg.C), np.float32)
    for c in range(8):
        b, half = c // 2, c % 2
        out[b, half::2, :] = res.results[c]["out_own"]
    return out


# version 20
# speedup vs baseline: 1.0772x; 1.0772x over previous
import numpy as np
import ml_dtypes
from contextlib import ExitStack

import concourse.bass as bass
import concourse.mybir as mybir
from concourse.bass_utils import run_bass_kernel_spmd

F32 = mybir.dt.float32
BF16 = mybir.dt.bfloat16
AF = mybir.ActivationFunctionType
ALU = mybir.AluOpType
AX = mybir.AxisListType
P = 128
BFDT = ml_dtypes.bfloat16


def chunks(total, w=512):
    out, o = [], 0
    while o < total:
        out.append((o, min(w, total - o)))
        o += w
    return out


class CFG:
    def __init__(self, C=2048, T=2048, H=16, F=8192, R=32, NB=4):
        self.C, self.T, self.H, self.F, self.R, self.NB = C, T, H, F, R, NB
        self.DH = 128
        assert C // H == self.DH and C % P == 0 and T % 256 == 0
        self.CT = C // P
        self.TT = T // P
        self.OT = T // 2 // P
        self.OWN = T // 2
        self.FT = F // P
        self.own_chunks = chunks(self.OWN)
        self.all_chunks = chunks(T)



def xpos_tables(cfg, half):
    C, T = cfg.C, cfg.T
    i = np.arange(C // 2, dtype=np.float64)
    dim_scale = (2 * i + 0.4 * C) / (1.4 * C)
    inv_freq = 1.0 / 10000.0 ** (2 * i / C)
    t = np.arange(T, dtype=np.float64)
    scale = dim_scale[None, :] ** (t[:, None] / 512.0)
    sinu = t[:, None] * inv_freq[None, :]
    sinE = np.repeat(np.sin(sinu) * scale, 2, axis=1)
    cosE = np.repeat(np.cos(sinu) * scale, 2, axis=1)
    h_of_c = np.arange(C) // cfg.DH
    lg = np.log1p(-(2.0 ** (-5.0 - h_of_c.astype(np.float64))))
    gq = np.exp(t[:, None] * lg[None, :]) * cfg.DH ** -0.5
    gk = np.exp(-t[:, None] * lg[None, :])
    own = np.arange(cfg.OWN) * 2 + half
    cq = np.ascontiguousarray((cosE[own] * gq[own]).T).astype(np.float32)
    sq = np.ascontiguousarray((sinE[own] * gq[own]).T).astype(np.float32)
    ck = np.ascontiguousarray((cosE * gk).T).astype(np.float32)
    sk = np.ascontiguousarray((sinE * gk).T).astype(np.float32)
    return cq, sq, ck, sk


def host_consts(cfg, half):
    i = np.arange(P)[:, None]
    j = np.arange(256)[None, :]
    keep = (2 * i + half) >= j
    ltri = keep.astype(BFDT)
    jl = np.arange(P)[:, None]
    ii = np.arange(P)[None, :]
    ltriT = np.concatenate([(2 * ii + half >= jl).astype(BFDT),
                            (2 * ii + half >= jl + 128).astype(BFDT)], axis=1)
    eye = np.eye(P, dtype=BFDT)
    pmatT = np.zeros((P, P), np.float32)
    for q in range(P // 2):
        pmatT[2 * q + 1, 2 * q] = -1.0
        pmatT[2 * q, 2 * q + 1] = 1.0
    return ltri, ltriT, eye, pmatT.astype(BFDT)


def cvec(v, cfg):
    return np.ascontiguousarray(np.asarray(v, np.float32).reshape(cfg.CT, P).T)



class Builder:
    def __init__(self, cfg, dbg=()):
        self.cfg = cfg
        self.dbg = set(dbg)
        self.nc = bass.Bass("TRN2", target_bir_lowering=False)
        self.es = ExitStack()
        self.counts = {}
        self.sems = {}
        self.dbg_guard = []
        self.dsems = {}
        self.mem_guard = []

    def sem(self, name):
        h = self.es.enter_context(self.nc.semaphore(name))
        self.counts[name] = 0
        self.sems[name] = h
        return h

    def inc(self, inst, sem, n=1):
        inst.then_inc(sem, n)
        self.counts[sem.name] += n
        return self.counts[sem.name]

    def tok(self, sem):
        return (sem, self.counts[sem.name])

    def wt(self, eng, *toks):
        for sem, v in list(toks) + self.dbg_guard:
            if v > 0:
                eng.wait_ge(sem, v)

    def sbuf(self, name, shape, dtype, es=None):
        return (es or self.es).enter_context(
            self.nc.sbuf_tensor(name, list(shape), dtype))

    def psum(self, name, shape, dtype=F32):
        return self.es.enter_context(self.nc.psum_tensor(name, list(shape), dtype))

    def dsem(self, stream):
        if stream not in self.dsems:
            self.dsems[stream] = self.sem("d_" + stream)
        return self.dsems[stream]

    def dma(self, eng, stream, out, in_):
        sem = self.dsem(stream)
        cnt = self.counts[sem.name]
        if cnt > 0:
            eng.wait_ge(sem, cnt)
        self.inc(eng.dma_start(out=out, in_=in_), sem, 16)
        return self.tok(sem)

    def pump_step(self, n=1):
        for _ in range(n):
            if self.pump:
                self.pump.pop(0)()

    def pump_drain(self):
        while self.pump:
            self.pump.pop(0)()

    def T(self, fn): fn(self.nc.tensor)
    def V(self, fn): fn(self.nc.vector)
    def S(self, fn): fn(self.nc.scalar)
    def G(self, fn): fn(self.nc.gpsimd)
    def Y(self, fn): fn(self.nc.sync)

    def build(self):
        cfg = self.cfg
        nc = self.nc
        C, T, H, R, F = cfg.C, cfg.T, cfg.H, cfg.R, cfg.F
        CT, TT, OT, OWN, FT = cfg.CT, cfg.TT, cfg.OT, cfg.OWN, cfg.FT
        self.out_names = ["out_own"]

        din = lambda n, s: nc.dram_tensor(n, s, F32, kind="ExternalInput").ap()
        dbf = lambda n, s: nc.dram_tensor(n, s, BF16, kind="ExternalInput").ap()
        dint = lambda n, s: nc.dram_tensor(n, s, BF16, kind="Internal").ap()

        x_all = din("x_all", [T, C])
        x_own = din("x_own", [OWN, C])
        w_qkv = din("w_qkv", [C, 3 * C])
        w_g = din("w_g", [C, C])
        w_proj = din("w_proj", [C, C])
        wqkvT_f = din("wqkvT_f", [C, 3 * C])
        woT = din("woT", [C, C])
        ff1T = din("ff1T", [C, F])
        ff2T = din("ff2T", [F, C])
        w_downT = din("w_downT", [C, R])
        w_upT = din("w_upT", [R, C])
        wrT = din("wrT", [C, 3])
        cvecs = din("cvecs", [P, 7 * CT])
        cq_d = din("cq", [C, OWN]); sq_d = din("sq", [C, OWN])
        ck_d = din("ck", [C, T]); sk_d = din("sk", [C, T])
        ltri_d = dbf("ltri", [P, 256])
        ltriT_d = dbf("ltriT", [P, 256])
        eye_d = dbf("eye", [P, P])
        pmatT_d = dbf("pmatT", [P, P])
        out_own = nc.dram_tensor("out_own", [OWN, C], F32, kind="ExternalOutput").ap()
        yn_sp = dint("yn_sp", [C, OWN])
        s_sp = dint("s_sp", [C, OWN])
        ctx_sp = dint("ctx_sp", [C, OWN])
        x2_sp = dint("x2_sp", [C, OWN])
        hg_sp = dint("hg_sp", [F, OWN])

        ff1T_bf = dint("ff1T_bf", [C, F])
        ff2T_bf = dint("ff2T_bf", [F, C])

        r3 = lambda ap: ap.rearrange("(a p) n -> p a n", p=P)
        self.wqkv_r = r3(w_qkv)
        self.wg_r = r3(w_g)
        self.wproj_r = r3(w_proj)
        self.wqkvTf_r = r3(wqkvT_f)
        self.woT_r = r3(woT)
        self.ff1T_r = r3(ff1T)
        self.ff2T_r = r3(ff2T)
        self.ff1Tbf_r = r3(ff1T_bf)
        self.ff2Tbf_r = r3(ff2T_bf)
        wdT_r = r3(w_downT)
        wrT_r = r3(wrT)
        xall_r = x_all.rearrange("(tt p) c -> tt p c", p=P)
        xown_r = x_own.rearrange("(rt p) c -> rt p c", p=P)
        outown_r = out_own.rearrange("(rt p) c -> rt p c", p=P)
        slab = lambda ap: ap.rearrange("(h p) t -> h p t", p=P)
        self.cq_r, self.sq_r = slab(cq_d), slab(sq_d)
        self.ck_r, self.sk_r = slab(ck_d), slab(sk_d)
        self.yn_sp_r = slab(yn_sp)
        self.s_sp_r = slab(s_sp)
        self.ctx_sp_r = slab(ctx_sp)
        self.x2_sp_r = slab(x2_sp)
        hg_sp_w = hg_sp.rearrange("(ft p) t -> ft p t", p=P)
        hg_sp_r = hg_sp.rearrange("(ft p) t -> p ft t", p=P)

        sqd = self.sem("sqd"); sgd = self.sem("sgd")
        s_t = self.sem("s_t"); s_v = self.sem("s_v")
        s_s = self.sem("s_s"); s_p = self.sem("s_p")
        self.sqd, self.sgd, self.s_t, self.s_v, self.s_s, self.s_p = \
            sqd, sgd, s_t, s_v, s_s, s_p

        eye_sb = self.sbuf("eye_sb", [P, P], BF16)
        pmatT_sb = self.sbuf("pmatT_sb", [P, P], BF16)
        ltri_sb = self.sbuf("ltri_sb", [P, 256], BF16)
        ltriT_sb = self.sbuf("ltriT_sb", [P, 256], BF16)
        self.ltriT_sb = ltriT_sb
        ones_col = self.sbuf("ones_col", [P, 1], BF16)
        self.ones_row_f_early = self.sbuf("ones_row_f", [P, P], F32)
        self.ones_row_f = self.ones_row_f_early
        cv_sb = self.sbuf("cv_sb", [P, 7 * CT], F32)
        wrT_sb = self.sbuf("wrT_sb", [P, CT * 3], F32)
        wrhi = self.sbuf("wrhi", [P, CT * 3], BF16)
        wrlo = self.sbuf("wrlo", [P, CT * 3], BF16)
        wdT_sb = self.sbuf("wdT_sb", [P, CT * R], BF16)
        xn_ownT = self.sbuf("xn_ownT", [P, CT, OWN], BF16)
        xn_allT = self.sbuf("xn_allT", [P, CT, T], BF16)
        masks_bf = self.sbuf("masks_bf", [1, 2 * OWN], BF16)
        self.masks_bf = masks_bf
        ones_row_bf = self.sbuf("ones_row_bf", [1, P], BF16)
        self.ones_row_bf = ones_row_bf
        lr_sb = self.sbuf("lr_sb", [R, OWN], BF16)
        self.lr_sb = lr_sb
        self.eye_sb, self.pmatT_sb, self.ltri_sb = eye_sb, pmatT_sb, ltri_sb
        self.ones_col = ones_col
        self.xn_ownT, self.xn_allT = xn_ownT, xn_allT
        cvv = {nm: cv_sb[:, k * CT:(k + 1) * CT]
               for k, nm in enumerate(["gnw", "gnb", "ln1s", "ln1b", "ln2s", "ln2b", "sbias"])}

        pbA = self.psum("pbA", [P, 512]); pbB = self.psum("pbB", [P, 512])
        pbC = self.psum("pbC", [P, 512]); pbD = self.psum("pbD", [P, 512])
        pbY = self.psum("pbY", [P, 512]); pbS = self.psum("pbS", [P, 512])
        pbT = self.psum("pbT", [P, 1024], BF16)
        pbT2 = self.psum("pbT2", [P, 1024], BF16)
        self.pb = dict(A=pbA, B=pbB, C=pbC, D=pbD, Y=pbY, S=pbS, T=pbT, T2=pbT2)
        self.bfree = {k: [] for k in self.pb}

        self.block = self.es.enter_context(nc.Block())

        def _emit_all(_carrier_engine):
            def _(s):
                self.dma(s, "consts", ltri_sb[:], ltri_d[:])
                self.dma(s, "consts", ltriT_sb[:], ltriT_d[:])
                self.dma(s, "consts", eye_sb[:], eye_d[:])
                self.dma(s, "consts", pmatT_sb[:], pmatT_d[:])
                self.dma(s, "consts", cv_sb[:], cvecs[:])
                self.dma(s, "consts", wrT_sb.ap().rearrange("p (a n) -> p a n", n=3),
                         wrT_r[:, :, :])
            self.Y(_)
            consts_sq = self.tok(self.dsem("consts"))

            def _(g):
                g.memset(ones_col[:], 1.0)
                g.memset(self.ones_row_f_early[:], 1.0)
                g.memset(ones_row_bf[:], 1.0)
                self.inc(g.memset(wrhi[:], 0.0), s_p)
                self.dma(g, "gconsts", wdT_sb.ap().rearrange("p (a n) -> p a n", n=R),
                         wdT_r[:, :, :])
            self.G(_)
            consts_sg = self.tok(self.dsem("gconsts"))
            consts_sp = self.tok(s_p)
            self.consts_sq, self.consts_sg, self.consts_sp = consts_sq, consts_sg, consts_sp

            self.pc = {}
            self.pump = []

            def _mkcast(stream, dst, srcw, r0, r1, key):
                def go():
                    def _(g):
                        self.dma(g, stream, dst[r0:r1, :], srcw[r0:r1, :])
                    self.G(_)
                    if key is not None:
                        self.pc[key] = self.tok(self.dsem(stream))
                return go

            def _fill(stream, mir, srcw, key, nch):
                K = mir.shape[0]
                nch = min(nch, K)
                bounds = [K * i // nch for i in range(nch + 1)]
                for i in range(nch):
                    self.pump.append(_mkcast(stream, mir, srcw,
                                             bounds[i], bounds[i + 1],
                                             key if i == nch - 1 else None))
            _fill("pc1", ff1T_bf, ff1T, "ff1", 8)
            _fill("pc1", ff2T_bf, ff2T, "ff2", 8)

            def _(v):
                self.wt(v, consts_sq, consts_sp)
                self.inc(v.tensor_copy(wrhi[:], wrT_sb[:]), s_v)
                self.wt(v, self.tok(s_v))
                self.inc(v.tensor_tensor(wrlo[:], wrT_sb[:], wrhi[:], op=ALU.subtract), s_v)
            self.V(_)
            wr_tok = self.tok(s_v)

            st1 = self.sbuf("p0s1", [P, 1], F32)
            st2 = self.sbuf("p0s2", [P, 1], F32)
            stm = self.sbuf("p0sm", [P, 1], F32)
            ph0_es = ExitStack()
            xa = [self.sbuf(f"p0x{i}", [P, C], F32, es=ph0_es) for i in range(2)]
            xn_tok = [self.sbuf(f"p0n{i}", [P, C], BF16, es=ph0_es) for i in range(2)]
            lo_tok = [self.sbuf(f"p0l{i}", [P, C], BF16, es=ph0_es) for i in range(2)]
            xnloT = self.sbuf("xnloT", [P, CT, OWN], BF16, es=ph0_es)
            l_sb = self.sbuf("l_sb", [1, 3 * OWN], F32, es=ph0_es)
            mtmp = self.sbuf("mtmp", [1, 2 * OWN], F32, es=ph0_es)

            buf_guard = [[], []]
            self.ln_guard = []

            def ln_tile(kind, idx):
                buf = idx % 2
                xt = xa[buf]

                ldtok_box = []

                def _(s):
                    self.wt(s, *buf_guard[buf])
                    src = xall_r[idx] if kind == "all" else xown_r[idx]
                    ldtok_box.append(self.dma(s, f"xload{buf}", xt[:], src))
                self.Y(_)
                ldtok = ldtok_box[0]

                def _(v):
                    self.wt(v, ldtok, *self.ln_guard)
                    self.inc(v.tensor_reduce(st1[:], xt[:], axis=AX.X, op=ALU.add), s_v)
                self.V(_)
                t_mean = self.tok(s_v)

                def _(s):
                    self.wt(s, ldtok, *self.ln_guard)
                    self.inc(s.activation(lo_tok[buf][:], xt[:], AF.Square,
                                          accum_out=st2[:]), s_s)
                self.S(_)
                t_sq = self.tok(s_s)

                def _(v):
                    self.wt(v, t_mean, t_sq)
                    self.inc(v.tensor_scalar(stm[:], st1[:], 1.0 / C, None, op0=ALU.mult), s_v)
                    self.wt(v, self.tok(s_v))
                    self.inc(v.tensor_tensor(st1[:], stm[:], stm[:], op=ALU.mult), s_v)
                    self.wt(v, self.tok(s_v))
                    self.inc(v.tensor_scalar(st2[:], st2[:], 1.0 / C, None, op0=ALU.mult), s_v)
                    self.wt(v, self.tok(s_v))
                    self.inc(v.tensor_tensor(st2[:], st2[:], st1[:], op=ALU.subtract), s_v)
                    self.wt(v, self.tok(s_v))
                    self.inc(v.tensor_scalar(st2[:], st2[:], 1e-5, None, op0=ALU.add), s_v)
                self.V(_)
                t_rv = self.tok(s_v)

                def _(s):
                    self.wt(s, t_rv)
                    self.inc(s.activation(st1[:], st2[:], AF.Ln), s_s)
                    self.wt(s, self.tok(s_s))
                    self.inc(s.activation(st2[:], st1[:], AF.Exp, scale=-0.5), s_s)
                self.S(_)
                t_rstd = self.tok(s_s)

                def _(v):
                    self.wt(v, t_rstd)
                    self.inc(v.tensor_scalar(xn_tok[buf][:], xt[:], stm[:], st2[:],
                                             op0=ALU.subtract, op1=ALU.mult), s_v)
                    if kind == "own":
                        self.inc(v.tensor_scalar(xt[:], xt[:], stm[:], st2[:],
                                                 op0=ALU.subtract, op1=ALU.mult), s_v)
                        self.wt(v, self.tok(s_v))
                        self.inc(v.tensor_tensor(lo_tok[buf][:], xt[:], xn_tok[buf][:],
                                                 op=ALU.subtract), s_v)
                self.V(_)
                self.ln_guard = [self.tok(s_v)]
                return self.tok(s_v)

            tpar = [0]

            def tp_tile(src, t_ready, dest, dw, col):
                toks = []
                for g in range((CT + 7) // 8):
                    nk = min(8, CT - g * 8)
                    key = "T" if tpar[0] % 2 == 0 else "T2"
                    tpar[0] += 1
                    pb = self.pb[key]

                    def _(t, g=g, nk=nk, pb=pb, key=key):
                        self.wt(t, t_ready, self.consts_sq, *self.bfree[key])
                        for k in range(nk):
                            ct = g * 8 + k
                            inst = t.transpose(pb[:, k * P:(k + 1) * P],
                                               src[:, ct * P:(ct + 1) * P], eye_sb[:])
                        self.inc(inst, s_t)
                    self.T(_)
                    tg = self.tok(s_t)

                    def _(v, g=g, nk=nk, pb=pb, key=key):
                        self.wt(v, tg)
                        for k in range(nk):
                            ct = g * 8 + k
                            iv = v.tensor_copy(dest[:, ct, col:col + P], pb[:, k * P:(k + 1) * P])
                        self.inc(iv, s_v)
                    self.V(_)
                    self.bfree[key] = [self.tok(s_v)]
                    toks.append(self.tok(s_v))
                return toks[-1]

            for idx in range(TT):
                t_r = ln_tile("all", idx)
                e = tp_tile(xn_tok[idx % 2], t_r, xn_allT, T, idx * P)
                buf_guard[idx % 2] = [e]
            for idx in range(OT):
                t_r = ln_tile("own", idx)
                e1 = tp_tile(xn_tok[idx % 2], t_r, xn_ownT, OWN, idx * P)
                e2 = tp_tile(lo_tok[idx % 2], t_r, xnloT, OWN, idx * P)
                buf_guard[idx % 2] = [e1, e2]
            ph0_v = self.tok(s_v)
            self.dbg_dump("xn_allT", xn_allT, [P, CT * T], [ph0_v])
            self.dbg_dump("xn_ownT", xn_ownT, [P, CT * OWN], [ph0_v])

            nprod = CT * 4
            for j in range(3):
                for (c0, cw) in cfg.own_chunks:
                    def _(t, j=j, c0=c0, cw=cw):
                        self.wt(t, ph0_v, wr_tok, *self.bfree["S"])
                        k = 0
                        for ct in range(CT):
                            for wsb, xsb in ((wrhi, xn_ownT), (wrhi, xnloT),
                                             (wrlo, xn_ownT), (wrlo, xnloT)):
                                inst = t.matmul(pbS[0:1, 0:cw],
                                                wsb[:, ct * 3 + j:ct * 3 + j + 1],
                                                xsb[:, ct, c0:c0 + cw],
                                                start=(k == 0), stop=(k == nprod - 1))
                                k += 1
                        self.inc(inst, s_t)
                    self.T(_)
                    tg = self.tok(s_t)

                    def _(v, j=j, c0=c0, cw=cw, tg=tg):
                        self.wt(v, tg)
                        self.inc(v.tensor_copy(l_sb[0:1, j * OWN + c0:j * OWN + c0 + cw],
                                               pbS[0:1, 0:cw]), s_v)
                    self.V(_)
                    self.bfree["S"] = [self.tok(s_v)]

            l0 = l_sb[0:1, 0 * OWN:1 * OWN]
            l1 = l_sb[0:1, 1 * OWN:2 * OWN]
            l2 = l_sb[0:1, 2 * OWN:3 * OWN]
            m0 = masks_bf[0:1, 0 * OWN:1 * OWN]
            m1 = masks_bf[0:1, 1 * OWN:2 * OWN]
            ta = mtmp[0:1, 0:OWN]
            tb = mtmp[0:1, OWN:2 * OWN]

            def _(v):
                self.wt(v, self.tok(s_v))
                self.inc(v.tensor_tensor(ta, l0, l1, op=ALU.is_ge), s_v)
                self.inc(v.tensor_tensor(tb, l0, l2, op=ALU.is_ge), s_v)
                self.wt(v, self.tok(s_v))
                self.inc(v.tensor_tensor(m0, ta, tb, op=ALU.mult), s_v)
                self.wt(v, self.tok(s_v))
                self.inc(v.tensor_tensor(ta, l1, l2, op=ALU.is_ge), s_v)
                self.wt(v, self.tok(s_v))
                self.inc(v.tensor_scalar(tb, m0, -1.0, 1.0, op0=ALU.mult, op1=ALU.add), s_v)
                self.wt(v, self.tok(s_v))
                self.inc(v.tensor_tensor(m1, tb, ta, op=ALU.mult), s_v)
            self.V(_)
            self.masks_tok = self.tok(s_v)
            self.m0, self.m1 = m0, m1
            if "masks" in self.dbg:
                mdump = self.nc.dram_tensor("dbg_masks", [1, 2 * OWN], F32,
                                            kind="ExternalOutput").ap()
                self.out_names.append("dbg_masks")

                def _(g):
                    self.wt(g, self.masks_tok)
                    self.dbg_tok = self.dma(g, "dbg", mdump[:], masks_bf[0:1, 0:2 * OWN])
                self.G(_)
                self.dbg_guard = [self.dbg_tok]

            for (c0, cw) in cfg.own_chunks:
                def _(t, c0=c0, cw=cw):
                    self.wt(t, ph0_v, consts_sg, *self.bfree["S"])
                    for ct in range(CT):
                        inst = t.matmul(pbS[0:R, 0:cw], wdT_sb[:, ct * R:(ct + 1) * R],
                                        xn_ownT[:, ct, c0:c0 + cw],
                                        start=(ct == 0), stop=(ct == CT - 1))
                    self.inc(inst, s_t)
                self.T(_)
                tg = self.tok(s_t)

                def _(v, c0=c0, cw=cw, tg=tg):
                    self.wt(v, tg)
                    self.inc(v.tensor_copy(lr_sb[:, c0:c0 + cw], pbS[0:R, 0:cw]), s_v)
                self.V(_)
                self.bfree["S"] = [self.tok(s_v)]
            self.lr_tok = self.tok(s_v)

            ph0_end = (self.tok(s_t), self.tok(s_v), self.tok(s_s))
            ph0_es.close()
            self.attn_es = ExitStack()
            self.tab_es = ExitStack()
            self.mk_attn_bufs()
            self.cvv = cvv
            self.mem_guard = list(ph0_end)
            self.emit_attention("ret", xn_ownT, yn_sp=self.yn_sp_r, ready=[ph0_v])
            self.dbg_dump("yn", None, [C, OWN], [], dram=yn_sp,
                          guards_dma=[self.attn_done[3]])

            ret_end = (self.tok(s_t), self.tok(s_v), self.tok(s_s))
            self.tab_es.close()
            self.bufA_es = ExitStack()
            bufA = self.sbuf("bufA", [P, CT, OWN], BF16, es=self.bufA_es)
            self.bufA = bufA
            self.mem_guard = list(ret_end)

            self.row_es = ExitStack()
            self.rowbuf = self.sbuf("rowbuf", [P, 3 * OWN], F32, es=self.row_es)
            self.qn_row = self.rowbuf[0:1, 0:OWN]
            self.nc_row = self.rowbuf[0:1, OWN:2 * OWN]

            ln_ready = list(ret_end)
            late_ready = []
            sa0, ta0 = self.ln_units(xn_allT, xn_allT, 0, OWN, cvv["ln1s"],
                                     cvv["ln1b"], ln_ready,
                                     scr_pair=(self.q2c, self.k2c),
                                     rowp=0)
            sa1, ta1 = self.ln_units(xn_allT, xn_allT, OWN, T - OWN, cvv["ln1s"],
                                     cvv["ln1b"], ln_ready,
                                     scr_pair=(self.q2c, self.k2c),
                                     rowp=32)
            so, to = self.ln_units(xn_ownT, bufA, 0, OWN, cvv["ln1s"],
                                   cvv["ln1b"], ln_ready,
                                   scr_pair=(self.q2c, self.k2c),
                                   apply_ready=late_ready, rowp=64)
            self.emit_gate_proj(cvv, fill_gate=sa0 + sa1 + so,
                                fill_proj=ta0 + ta1)
            self.dbg_dump("s", None, [C, OWN], [], dram=s_sp, guards_dma=list(self.proj_sqd))
            ln1a_tok = self._ln_tok
            late_ready.append(self.proj_tok)
            for fn in to:
                fn()
            ln1o_tok = self._ln_tok
            self.dbg_dump("xn1_allT", xn_allT, [P, CT * T], [ln1a_tok])
            self.dbg_dump("xn1_ownT", bufA, [P, CT * OWN], [ln1o_tok])

            self.emit_attention("ffn", bufA, ctx_sp=self.ctx_sp_r,
                                ready=[ln1a_tok, ln1o_tok])
            self.pump_drain()
            self.dbg_dump("ctx", None, [C, OWN], [], dram=ctx_sp,
                          guards_dma=[self.attn_done[3]])

            self.emit_wo_x2_ln2(cvv)

            self.emit_ff_stageA(hg_sp_w)

            self.dbg_dump("hg", None, [F, OWN], [], dram=hg_sp,
                          guards_dma=[self.tok(self.dsem("hgsp0")),
                                      self.tok(self.dsem("hgsp1"))])
            stA_end = (self.tok(s_t), self.tok(s_v), self.tok(s_s))
            self.row_es.close()
            self.bufA_es.close()
            self.attn_es.close()
            self.mem_guard = list(stA_end) + list(self.stageA_sqd)
            self.w_upT_d = w_upT
            self.emit_stageB_combine(hg_sp_r, cvv, xown_r, outown_r)

            def _(s):
                for nm, sem in self.dsems.items():
                    self.wt(s, self.tok(sem))
            self.Y(_)

        self.block.sync(_emit_all)
        self.sb_es.close()
        self.es.close()
        return self.nc

    def dbg_dump(self, name, sb, shape, guards, dram=None, guards_dma=()):
        if name not in self.dbg:
            return
        ap = self.nc.dram_tensor("dbg_" + name, list(shape), F32, kind="ExternalOutput").ap()
        self.out_names.append("dbg_" + name)
        if dram is not None:
            src = dram[:]
        else:
            src = sb.ap()
            if len(src.shape) == 3:
                src = src.rearrange("p a n -> p (a n)")
            else:
                src = sb[:]

        def _(g):
            for sem, v in list(guards) + list(guards_dma):
                if v > 0:
                    g.wait_ge(sem, v)
            self.dbg_tok = self.dma(g, "dbg", ap[:], src)
        self.G(_)
        self.dbg_guard = [self.dbg_tok]

    def mk_attn_bufs(self):
        cfg = self.cfg
        OWN, T, OT = cfg.OWN, cfg.T, cfg.OT
        aes, tes = self.attn_es, self.tab_es
        self.wq_h = [self.sbuf(f"wqh{i}", [P, cfg.CT, P], BF16, es=aes) for i in range(2)]
        self.wk_h = [self.sbuf(f"wkh{i}", [P, cfg.CT, P], BF16, es=aes) for i in range(2)]
        self.wv_h = [self.sbuf(f"wvh{i}", [P, cfg.CT, P], BF16, es=aes) for i in range(2)]
        self.qb_sb = self.sbuf("qb_sb", [P, OWN], BF16, es=aes)
        self.q2c = self.sbuf("q2c", [P, 512], BF16, es=aes)
        self.kb_sb = self.sbuf("kb_sb", [P, T], BF16, es=aes)
        self.k2c = self.sbuf("k2c", [P, 512], BF16, es=aes)
        self.vtmp = self.sbuf("vtmp", [P, 512], BF16, es=aes)
        self.vtmp2 = self.sbuf("vtmp2", [P, 512], BF16, es=aes)
        self.vtok = self.sbuf("vtok", [P, T], BF16, es=aes)
        self.sc_pair = [self.sbuf(f"scp{i}", [P, T], BF16, es=aes) for i in range(2)]
        self.scT_sb = self.sbuf("scT_sb", [P, T], BF16, es=aes)
        self.wt_pair = [self.sbuf(f"wtp{i}", [P, 256], BF16, es=aes) for i in range(2)]
        self.yslab = self.sbuf("yslab", [P, OWN], BF16, es=aes)
        self.ybuf = self.sbuf("ybuf", [P, OT * P], BF16, es=aes)
        self.ybuf2 = self.scT_sb[:, 0:OT * P]
        self.gstA = self.sbuf("gstA", [P, OT], F32, es=aes)
        self.gstB = self.sbuf("gstB", [P, OT], F32, es=aes)
        self.gstC = self.sbuf("gstC", [P, OT], F32, es=aes)
        self.den_sb = self.sbuf("den_sb", [P, 16], F32, es=aes)
        self.denall = self.sbuf("denall", [P, OT], F32, es=aes)
        self.mk2 = self.sbuf("mk2", [1, 1], F32, es=aes)
        self.mk2b = self.sbuf("mk2b", [1, 1], F32, es=aes)
        self.biasT = self.sbuf("biasT", [P, OT], F32, es=aes)
        self.sqtmp = self.vtmp
        self.lntmp = self.sbuf("lntmp", [P, 512], BF16, es=aes)

        self.qq_sb = self.sbuf("qq_sb", [P, OWN], BF16, es=tes)
        self.cq_sb = [self.sbuf(f"cqs{i}", [P, OWN], BF16, es=tes) for i in range(2)]
        self.sq_sb = [self.sbuf(f"sqs{i}", [P, OWN], BF16, es=tes) for i in range(2)]
        self.ck_sb = [self.sbuf(f"cks{i}", [P, T], BF16, es=tes) for i in range(2)]
        self.sk_sb = [self.sbuf(f"sks{i}", [P, T], BF16, es=tes) for i in range(2)]
        self.kk_sb = self.sbuf("kk_sb", [P, T], BF16, es=tes)
        self.g_sqtmp = []
        self.g_lntmp = []
        self.g_sil = []
        self.g_w2 = [[], []]
        self.g_w = [[], []]
        self.g_tab = [[], []]
        self.g_q = []
        self.g_k = []
        self.g_v = []
        self.g_scp = [[], []]
        self.g_wtp = [[], []]
        self.g_denp = [[], []]
        self.g_scT = []
        self.g_ybuf = []
        self.g_yb2 = []
        self.g_denall = []
        self.g_yslab = []
        self.g_bias = []

    def load_head(self, h, kind):
        cfg = self.cfg
        buf = h % 2
        C = cfg.C
        wsrc = self.wqkv_r if kind == "ret" else self.wqkvTf_r

        def _(g, h=h, buf=buf):
            self.wt(g, *self.g_w[buf], *self.g_w2[buf], *self.mem_guard)
            self.dma(g, f"wh{buf}", self.wq_h[buf].ap(),
                     wsrc[:, :, 0 * C + h * P:0 * C + (h + 1) * P])
            self.dma(g, f"wh{buf}", self.wk_h[buf].ap(),
                     wsrc[:, :, 1 * C + h * P:1 * C + (h + 1) * P])
            self.dma(g, f"wh{buf}", self.wv_h[buf].ap(),
                     wsrc[:, :, 2 * C + h * P:2 * C + (h + 1) * P])
        self.G(_)
        wtok = self.tok(self.dsem(f"wh{buf}"))
        ttok = None
        if kind == "ret":
            def _(g, h=h, buf=buf):
                self.wt(g, *self.g_tab[buf], *self.mem_guard)
                self.dma(g, f"tab{buf}", self.cq_sb[buf][:], self.cq_r[h])
                self.dma(g, f"tab{buf}", self.sq_sb[buf][:], self.sq_r[h])
                self.dma(g, f"tab{buf}", self.ck_sb[buf][:], self.ck_r[h])
                self.dma(g, f"tab{buf}", self.sk_sb[buf][:], self.sk_r[h])
            self.G(_)
            ttok = self.tok(self.dsem(f"tab{buf}"))
        self.wload[h] = wtok
        self.tload[h] = ttok

    def emit_attention(self, kind, xown3, yn_sp=None, ctx_sp=None, ready=()):
        cfg = self.cfg
        CT, OT, OWN, T, H = cfg.CT, cfg.OT, cfg.OWN, cfg.T, cfg.H
        s_t, s_v, s_s = self.s_t, self.s_v, self.s_s
        pb = self.pb
        qscale = 1.0 if kind == "ret" else cfg.DH ** -0.5
        spill = yn_sp if kind == "ret" else ctx_sp
        self.g_den = []
        self.g_rden = []

        self.wload = [None] * H
        self.tload = [None] * H
        self.load_head(0, kind)

        for h in range(H):
            buf = h % 2
            if h + 1 < H:
                self.load_head(h + 1, kind)
            if kind == "ffn":
                self.pump_step()

            for ci, (c0, cw) in enumerate(cfg.own_chunks):
                key = "A" if ci % 2 == 0 else "B"

                def _(t, c0=c0, cw=cw, key=key, buf=buf):
                    self.wt(t, self.wload[h], *ready, *self.bfree[key], *self.g_q)
                    for ct in range(CT):
                        inst = t.matmul(pb[key][:, 0:cw], self.wq_h[buf][:, ct, :],
                                        xown3[:, ct, c0:c0 + cw],
                                        start=(ct == 0), stop=(ct == CT - 1))
                    self.inc(inst, s_t)
                self.T(_)
                tg = self.tok(s_t)
                if ci % 2 == 0 or kind == "ffn":
                    def _(s, c0=c0, cw=cw, key=key, tg=tg):
                        self.wt(s, tg)
                        self.inc(s.activation(self.qb_sb[:, c0:c0 + cw], pb[key][:, 0:cw],
                                              AF.Identity, scale=qscale), s_s)
                    self.S(_)
                    self.bfree[key] = [self.tok(s_s)]
                else:
                    def _(v, c0=c0, cw=cw, key=key, tg=tg):
                        self.wt(v, tg)
                        self.inc(v.tensor_scalar(self.qb_sb[:, c0:c0 + cw],
                                                 pb[key][:, 0:cw], qscale, None,
                                                 op0=ALU.mult), s_v)
                    self.V(_)
                    self.bfree[key] = [self.tok(s_v)]
            q_ev = (self.tok(s_s), self.tok(s_v))
            self.g_q = []

            if kind == "ret":
                for ci, (c0, cw) in enumerate(cfg.own_chunks):
                    key = "C" if ci % 2 == 0 else "D"

                    def _(t, c0=c0, cw=cw, key=key):
                        self.wt(t, *q_ev, *self.bfree[key])
                        inst = t.matmul(pb[key][:, 0:cw], self.pmatT_sb[:],
                                        self.qb_sb[:, c0:c0 + cw], start=True, stop=True)
                        self.inc(inst, s_t)
                    self.T(_)
                    tg = self.tok(s_t)

                    def _(v, c0=c0, cw=cw, key=key, tg=tg, buf=buf):
                        self.wt(v, tg, self.tload[h], self.tok(s_v))
                        self.inc(v.tensor_tensor(self.qq_sb[:, c0:c0 + cw],
                                                 self.qb_sb[:, c0:c0 + cw],
                                                 self.cq_sb[buf][:, c0:c0 + cw], op=ALU.mult), s_v)
                        self.inc(v.tensor_tensor(self.q2c[:, 0:cw], pb[key][:, 0:cw],
                                                 self.sq_sb[buf][:, c0:c0 + cw], op=ALU.mult), s_v)
                        self.wt(v, self.tok(s_v))
                        self.inc(v.tensor_tensor(self.qq_sb[:, c0:c0 + cw],
                                                 self.qq_sb[:, c0:c0 + cw],
                                                 self.q2c[:, 0:cw], op=ALU.add), s_v)
                    self.V(_)
                    self.bfree[key] = [self.tok(s_v)]
                q_src, q_done = self.qq_sb, [self.tok(s_v)]
            else:
                q_src, q_done = self.qb_sb, list(q_ev)

            for ci, (c0, cw) in enumerate(cfg.all_chunks):
                key = "A" if ci % 2 == 0 else "B"

                def _(t, c0=c0, cw=cw, key=key, buf=buf):
                    self.wt(t, self.wload[h], *ready, *self.bfree[key], *self.g_k)
                    for ct in range(CT):
                        inst = t.matmul(pb[key][:, 0:cw], self.wk_h[buf][:, ct, :],
                                        self.xn_allT[:, ct, c0:c0 + cw],
                                        start=(ct == 0), stop=(ct == CT - 1))
                    self.inc(inst, s_t)
                self.T(_)
                tg = self.tok(s_t)
                if ci % 2 == 0:
                    def _(s, c0=c0, cw=cw, key=key, tg=tg):
                        self.wt(s, tg)
                        self.inc(s.activation(self.kb_sb[:, c0:c0 + cw], pb[key][:, 0:cw],
                                              AF.Identity), s_s)
                    self.S(_)
                    self.bfree[key] = [self.tok(s_s)]
                else:
                    def _(v, c0=c0, cw=cw, key=key, tg=tg):
                        self.wt(v, tg)
                        self.inc(v.tensor_copy(self.kb_sb[:, c0:c0 + cw],
                                               pb[key][:, 0:cw]), s_v)
                    self.V(_)
                    self.bfree[key] = [self.tok(s_v)]
            k_ev = (self.tok(s_s), self.tok(s_v))
            self.g_k = []
            if kind == "ret":
                for ci, (c0, cw) in enumerate(cfg.all_chunks):
                    key = "C" if ci % 2 == 0 else "D"

                    def _(t, c0=c0, cw=cw, key=key):
                        self.wt(t, *k_ev, *self.bfree[key])
                        inst = t.matmul(pb[key][:, 0:cw], self.pmatT_sb[:],
                                        self.kb_sb[:, c0:c0 + cw], start=True, stop=True)
                        self.inc(inst, s_t)
                    self.T(_)
                    tg = self.tok(s_t)

                    def _(v, c0=c0, cw=cw, key=key, tg=tg, buf=buf):
                        self.wt(v, tg, self.tload[h], self.tok(s_v))
                        self.inc(v.tensor_tensor(self.kk_sb[:, c0:c0 + cw],
                                                 self.kb_sb[:, c0:c0 + cw],
                                                 self.ck_sb[buf][:, c0:c0 + cw], op=ALU.mult), s_v)
                        self.inc(v.tensor_tensor(self.k2c[:, 0:cw], pb[key][:, 0:cw],
                                                 self.sk_sb[buf][:, c0:c0 + cw], op=ALU.mult), s_v)
                        self.wt(v, self.tok(s_v))
                        self.inc(v.tensor_tensor(self.kk_sb[:, c0:c0 + cw],
                                                 self.kk_sb[:, c0:c0 + cw],
                                                 self.k2c[:, 0:cw], op=ALU.add), s_v)
                    self.V(_)
                    self.bfree[key] = [self.tok(s_v)]
                k_src, k_done = self.kk_sb, [self.tok(s_v)]
            else:
                k_src, k_done = self.kb_sb, list(k_ev)

            vg = []
            tp_defer = [None]

            def _v_tp(c0, cw, tkey, ev, vt):
                def go():
                    def _(t, c0=c0, cw=cw, tkey=tkey, ev=ev, vt=vt):
                        self.wt(t, ev, *self.bfree[tkey])
                        for k in range(cw // P):
                            inst = t.transpose(pb[tkey][:, k * P:(k + 1) * P],
                                               vt[:, k * P:(k + 1) * P],
                                               self.eye_sb[:])
                        self.inc(inst, s_t)
                    self.T(_)
                    tg2 = self.tok(s_t)

                    def _(v, c0=c0, cw=cw, tkey=tkey, tg2=tg2):
                        self.wt(v, tg2)
                        self.inc(v.tensor_copy(self.vtok[:, c0:c0 + cw],
                                               pb[tkey][:, 0:cw]), s_v)
                    self.V(_)
                    self.bfree[tkey] = [self.tok(s_v)]
                    return [self.tok(s_v)]
                return go

            for ci, (c0, cw) in enumerate(cfg.all_chunks):
                key = "A" if ci % 2 == 0 else "B"
                vt = self.vtmp if ci % 2 == 0 else self.vtmp2

                def _(t, c0=c0, cw=cw, key=key, buf=buf):
                    self.wt(t, self.wload[h], *ready, *self.bfree[key], *self.g_v)
                    for ct in range(CT):
                        inst = t.matmul(pb[key][:, 0:cw], self.wv_h[buf][:, ct, :],
                                        self.xn_allT[:, ct, c0:c0 + cw],
                                        start=(ct == 0), stop=(ct == CT - 1))
                    self.inc(inst, s_t)
                self.T(_)
                tg = self.tok(s_t)
                if tp_defer[0] is not None:
                    vg = tp_defer[0]()

                def _(s, cw=cw, key=key, tg=tg, vt=vt, vgl=list(vg)):
                    self.wt(s, tg, *vgl)
                    self.inc(s.activation(vt[:, 0:cw], pb[key][:, 0:cw], AF.Identity), s_s)
                self.S(_)
                ev = self.tok(s_s)
                self.bfree[key] = [ev]
                tkey = "T" if ci % 2 == 0 else "T2"
                tp_defer[0] = _v_tp(c0, cw, tkey, ev, vt)
            vg = tp_defer[0]()
            v_done = self.tok(s_v)
            self.g_v = []

            if kind == "ffn":
                q2_guard = []
                for ci, (c0, cw) in enumerate(cfg.own_chunks):
                    def _(s, c0=c0, cw=cw, q2g=list(q2_guard)):
                        self.wt(s, *q_ev, *self.g_bias, *q2g)
                        self.inc(s.activation(self.q2c[:, 0:cw],
                                              self.qb_sb[:, c0:c0 + cw], AF.Square), s_s)
                    self.S(_)
                    q2tok = self.tok(s_s)

                    def _(t, c0=c0, cw=cw, q2tok=q2tok):
                        self.wt(t, q2tok, self.consts_sp, *self.bfree["S"])
                        inst = t.matmul(pb["S"][0:1, 0:cw], self.ones_col[:],
                                        self.q2c[:, 0:cw], start=True, stop=True)
                        self.inc(inst, s_t)
                    self.T(_)
                    tg = self.tok(s_t)
                    q2_guard = [tg]

                    def _(v, c0=c0, cw=cw, tg=tg):
                        self.wt(v, tg)
                        self.inc(v.tensor_copy(self.qn_row[:, c0:c0 + cw],
                                               pb["S"][0:1, 0:cw]), s_v)
                    self.V(_)
                    self.bfree["S"] = [self.tok(s_v)]
                k2_guard = []
                for ci, (c0, cw) in enumerate(cfg.all_chunks):
                    def _(s, c0=c0, cw=cw, k2g=list(k2_guard)):
                        self.wt(s, *k_ev, *k2g)
                        self.inc(s.activation(self.k2c[:, 0:cw],
                                              self.kb_sb[:, c0:c0 + cw], AF.Square), s_s)
                    self.S(_)
                    k2tok = self.tok(s_s)

                    def _(t, c0=c0, cw=cw, k2tok=k2tok):
                        self.wt(t, k2tok, self.consts_sp, *self.bfree["S"])
                        inst = t.matmul(pb["S"][0:1, 0:cw], self.ones_col[:],
                                        self.k2c[:, 0:cw], start=True, stop=True)
                        self.inc(inst, s_t)
                    self.T(_)
                    tg = self.tok(s_t)
                    k2_guard = [tg]

                    def _(v, c0=c0, cw=cw, tg=tg, ci=ci):
                        self.wt(v, tg, self.tok(s_v))
                        self.inc(v.tensor_reduce(self.mk2b[:], pb["S"][0:1, 0:cw],
                                                 axis=AX.X, op=ALU.max), s_v)
                        self.wt(v, self.tok(s_v))
                        if ci == 0:
                            self.inc(v.tensor_copy(self.mk2[:], self.mk2b[:]), s_v)
                        else:
                            self.inc(v.tensor_tensor(self.mk2[:], self.mk2[:],
                                                     self.mk2b[:], op=ALU.max), s_v)
                    self.V(_)
                    self.bfree["S"] = [self.tok(s_v)]

                def _(v):
                    self.wt(v, self.tok(s_v))
                    self.inc(v.tensor_scalar(self.nc_row, self.qn_row,
                                             self.mk2[:], None, op0=ALU.mult), s_v)
                self.V(_)

                def _(s):
                    self.wt(s, self.tok(s_v))
                    mtmp_row = self.rowbuf[0:1, 2 * OWN:3 * OWN]
                    self.inc(s.activation(mtmp_row, self.nc_row, AF.Ln), s_s)
                    self.wt(s, self.tok(s_s))
                    self.inc(s.activation(self.nc_row, mtmp_row, AF.Exp, scale=0.5), s_s)
                self.S(_)

                def _(v):
                    self.wt(v, self.tok(s_s))
                    self.inc(v.tensor_scalar(self.nc_row, self.nc_row, -1.0, None,
                                             op0=ALU.mult), s_v)
                self.V(_)
                ncr_tok = self.tok(s_v)

                def _(t):
                    self.wt(t, ncr_tok, *self.bfree["Y"])
                    for rt in range(OT):
                        inst = t.transpose(pb["Y"][:, rt:rt + 1],
                                           self.nc_row[:, rt * P:(rt + 1) * P],
                                           self.ones_row_f[0:1, 0:1])
                    self.inc(inst, s_t)
                self.T(_)
                tg = self.tok(s_t)

                def _(v, tg=tg):
                    self.wt(v, tg, *self.g_bias)
                    self.inc(v.tensor_copy(self.biasT[:], pb["Y"][:, 0:OT]), s_v)
                self.V(_)
                self.bfree["Y"] = [self.tok(s_v)]
                bias_tok = self.tok(s_v)
            else:
                bias_tok = None

            st_scb = [None] * OT
            st_win = [None] * OT
            st_den = [None] * OT
            ybanks = ("Y", "S")

            def stage_scores(rt):
                W = 256 * (rt + 1)
                mchunks = chunks(W)
                ech = chunks(W - 256) + [(W - 256, 256)]
                scp = self.sc_pair[rt % 2]
                for ci, (c0, cw) in enumerate(mchunks):
                    key = "ABC"[ci % 3]

                    def _(t, rt=rt, c0=c0, cw=cw, key=key):
                        self.wt(t, *q_done, *k_done, *self.bfree[key],
                                *self.g_scp[rt % 2])
                        inst = t.matmul(pb[key][:, 0:cw], q_src[:, rt * P:(rt + 1) * P],
                                        k_src[:, c0:c0 + cw], start=True, stop=True)
                        self.inc(inst, s_t)
                    self.T(_)
                    tg = self.tok(s_t)
                    if kind == "ret":
                        if ci % 2 == 0:
                            def _(s, rt=rt, c0=c0, cw=cw, key=key, tg=tg, scp=scp):
                                self.wt(s, tg)
                                self.inc(s.activation(scp[:, c0:c0 + cw],
                                                      pb[key][:, 0:cw], AF.Identity), s_s)
                            self.S(_)
                        else:
                            def _(v, rt=rt, c0=c0, cw=cw, key=key, tg=tg, scp=scp):
                                self.wt(v, tg)
                                self.inc(v.tensor_copy(scp[:, c0:c0 + cw],
                                                       pb[key][:, 0:cw]), s_v)
                            self.V(_)
                    else:
                        subs = [(e0, ew, k) for k, (e0, ew) in enumerate(ech)
                                if c0 <= e0 < c0 + cw]

                        def _(s, rt=rt, c0=c0, cw=cw, key=key, tg=tg, subs=subs, scp=scp):
                            self.wt(s, tg, bias_tok, *self.g_denp[rt % 2])
                            for (e0, ew, k) in subs:
                                self.inc(s.activation(
                                    scp[:, e0:e0 + ew],
                                    pb[key][:, e0 - c0:e0 - c0 + ew], AF.Exp,
                                    bias=self.biasT[:, rt:rt + 1],
                                    accum_out=self.den_sb[:, (rt % 2) * 8 + k:
                                                          (rt % 2) * 8 + k + 1]), s_s)
                        self.S(_)
                    self.bfree[key] = [self.tok(s_s) if kind == "ffn" or ci % 2 == 0
                                       else self.tok(s_v)]
                st_scb[rt] = (self.tok(s_s), self.tok(s_v))

                if kind == "ffn":
                    def _(v, rt=rt, W=W, scp=scp):
                        self.wt(v, *st_scb[rt], *self.g_wtp[rt % 2])
                        self.inc(v.tensor_tensor(self.wt_pair[rt % 2][:],
                                                 scp[:, W - 256:W],
                                                 self.ltri_sb[:], op=ALU.mult), s_v)
                    self.V(_)
                    st_win[rt] = self.tok(s_v)
                if kind == "ffn":
                    def _(v, rt=rt, W=W):
                        nech = len(chunks(W - 256)) + 1
                        self.wt(v, st_win[rt], *self.g_denall)
                        self.inc(v.tensor_reduce(
                            self.den_sb[:, (rt % 2) * 8 + nech - 1:(rt % 2) * 8 + nech],
                            self.wt_pair[rt % 2][:], axis=AX.X, op=ALU.add), s_v)
                        self.wt(v, self.tok(s_v))
                        self.inc(v.tensor_reduce(
                            self.denall[:, rt:rt + 1],
                            self.den_sb[:, (rt % 2) * 8:(rt % 2) * 8 + nech],
                            axis=AX.X, op=ALU.add), s_v)
                        st_den[rt] = self.tok(s_v)
                    self.V(_)

            def stage_consume(rt):
                W = 256 * (rt + 1)
                nst = W // P
                scp = self.sc_pair[rt % 2]
                ev_batch = []
                for g0 in range(0, nst, 8):
                    nk = min(8, nst - g0)
                    tkey = "T" if (g0 // 8) % 2 == 0 else "T2"

                    def _(t, rt=rt, g0=g0, nk=nk, tkey=tkey, scp=scp, nst=nst):
                        self.wt(t, *st_scb[rt], *self.bfree[tkey])
                        for k in range(nk):
                            st = g0 + k
                            inst = t.transpose(pb[tkey][:, k * P:(k + 1) * P],
                                               scp[:, st * P:(st + 1) * P],
                                               self.eye_sb[:])
                        self.inc(inst, s_t)
                    self.T(_)
                    tg = self.tok(s_t)

                    def _(v, g0=g0, nk=nk, tkey=tkey, tg=tg, nst=nst):
                        self.wt(v, tg, *self.g_scT, *self.g_yb2)
                        nw = max(0, (g0 + nk) - (nst - 2))
                        nc_ = nk - nw
                        if nc_ > 0:
                            iv = v.tensor_copy(self.scT_sb[:, g0 * P:(g0 + nc_) * P],
                                               pb[tkey][:, 0:nc_ * P])
                        if nw > 0:
                            iv = v.tensor_tensor(
                                self.scT_sb[:, (nst - 2) * P:nst * P],
                                pb[tkey][:, nc_ * P:nk * P],
                                self.ltriT_sb[:, 0:nw * P], op=ALU.mult)
                        self.inc(iv, s_v)
                    self.V(_)
                    self.bfree[tkey] = [self.tok(s_v)]
                    ev_batch.append(self.tok(s_v))
                scT_tok = self.tok(s_v)
                self.g_scp[rt % 2] = [self.tok(s_t)]
                self.g_wtp[rt % 2] = [self.tok(s_t)]

                ykey = ybanks[rt % 2]

                def _(t, rt=rt, nst=nst, ev_batch=ev_batch, ykey=ykey):
                    self.wt(t, v_done, *self.bfree[ykey])
                    for st in range(nst):
                        if st % 8 == 0:
                            self.wt(t, ev_batch[st // 8])
                        inst = t.matmul(pb[ykey][:, 0:P],
                                        self.scT_sb[:, st * P:(st + 1) * P],
                                        self.vtok[:, st * P:(st + 1) * P],
                                        start=(st == 0), stop=(st == nst - 1))
                    self.inc(inst, s_t)
                self.T(_)
                y_tok = self.tok(s_t)
                self.g_v.append(y_tok)
                self.g_scT = [y_tok]

                def _(v, rt=rt, y_tok=y_tok, ykey=ykey):
                    self.wt(v, y_tok, *self.g_ybuf)
                    self.inc(v.tensor_copy(self.ybuf[:, rt * P:(rt + 1) * P],
                                           pb[ykey][:, 0:P]), s_v)
                self.V(_)
                self.bfree[ykey] = [self.tok(s_v)]
                if kind == "ffn":
                    self.g_denp[rt % 2] = [st_den[rt]] if st_den[rt] else []

            stage_scores(0)
            for rt in range(OT):
                if rt + 1 < OT:
                    stage_scores(rt + 1)
                stage_consume(rt)
            yb_tok = self.tok(s_v)

            if kind == "ret":
                def _(s, yb_tok=yb_tok):
                    self.wt(s, yb_tok, *self.g_yb2)
                    self.inc(s.activation(self.ybuf2, self.ybuf[:], AF.Square), s_s)
                self.S(_)
                sq_tok2 = self.tok(s_s)
                yb3 = self.ybuf.ap().rearrange("p (r d) -> p r d", d=P)
                yb23 = self.scT_sb.ap().rearrange("p (r d) -> p r d", d=P)[:, 0:OT, :]

                def _(v, yb_tok=yb_tok, sq_tok2=sq_tok2):
                    self.wt(v, yb_tok, sq_tok2)
                    self.inc(v.tensor_reduce(self.gstA[:], yb3, axis=AX.X, op=ALU.add), s_v)
                    self.inc(v.tensor_reduce(self.gstB[:], yb23, axis=AX.X, op=ALU.add), s_v)
                    self.wt(v, self.tok(s_v))
                    self.inc(v.tensor_scalar(self.gstA[:], self.gstA[:], 1.0 / P, None,
                                             op0=ALU.mult), s_v)
                    self.inc(v.tensor_scalar(self.gstB[:], self.gstB[:], 1.0 / P, None,
                                             op0=ALU.mult), s_v)
                    self.wt(v, self.tok(s_v))
                    self.inc(v.tensor_tensor(self.gstC[:], self.gstA[:], self.gstA[:],
                                             op=ALU.mult), s_v)
                    self.wt(v, self.tok(s_v))
                    self.inc(v.tensor_tensor(self.gstB[:], self.gstB[:], self.gstC[:],
                                             op=ALU.subtract), s_v)
                    self.wt(v, self.tok(s_v))
                    self.inc(v.tensor_scalar(self.gstB[:], self.gstB[:], 1e-5, None,
                                             op0=ALU.add), s_v)
                self.V(_)

                def _(s):
                    self.wt(s, self.tok(s_v))
                    self.inc(s.activation(self.gstC[:], self.gstB[:], AF.Ln), s_s)
                    self.wt(s, self.tok(s_s))
                    self.inc(s.activation(self.gstB[:], self.gstC[:], AF.Exp,
                                          scale=-0.5), s_s)
                self.S(_)
                stat_tok = self.tok(s_s)

                def _(v, stat_tok=stat_tok):
                    self.wt(v, stat_tok)
                    for rt in range(OT):
                        iv = v.tensor_scalar(self.scT_sb[:, rt * P:(rt + 1) * P],
                                             self.ybuf[:, rt * P:(rt + 1) * P],
                                             self.gstA[:, rt:rt + 1],
                                             self.gstB[:, rt:rt + 1],
                                             op0=ALU.subtract, op1=ALU.mult)
                    self.inc(iv, s_v)
                self.V(_)
            else:
                def _(v, yb_tok=yb_tok):
                    self.wt(v, yb_tok, *self.g_yb2)
                    self.inc(v.reciprocal(self.gstA[:], self.denall[:]), s_v)
                    self.wt(v, self.tok(s_v))
                    for rt in range(OT):
                        iv = v.tensor_scalar(self.scT_sb[:, rt * P:(rt + 1) * P],
                                             self.ybuf[:, rt * P:(rt + 1) * P],
                                             self.gstA[:, rt:rt + 1], None,
                                             op0=ALU.mult)
                    self.inc(iv, s_v)
                self.V(_)
            yn_tok2 = self.tok(s_v)
            self.g_ybuf = [yn_tok2]
            self.g_denall = [yn_tok2]

            for g0 in range(0, OT, 8):
                nk = min(8, OT - g0)
                tkey = "T" if (g0 // 8) % 2 == 0 else "T2"

                def _(t, g0=g0, nk=nk, tkey=tkey, yn_tok2=yn_tok2):
                    self.wt(t, yn_tok2, *self.bfree[tkey])
                    for k in range(nk):
                        rt = g0 + k
                        inst = t.transpose(pb[tkey][:, k * P:(k + 1) * P],
                                           self.scT_sb[:, rt * P:(rt + 1) * P],
                                           self.eye_sb[:])
                    self.inc(inst, s_t)
                self.T(_)
                tg = self.tok(s_t)

                def _(v, g0=g0, nk=nk, tkey=tkey, tg=tg):
                    self.wt(v, tg, *self.g_yslab)
                    self.inc(v.tensor_copy(self.yslab[:, g0 * P:(g0 + nk) * P],
                                           pb[tkey][:, 0:nk * P]), s_v)
                self.V(_)
                self.bfree[tkey] = [self.tok(s_v)]
                self.g_yb2 = [self.tok(s_t)]
            slab_tok = self.tok(s_v)

            spt_box = []

            def _(s, h=h, slab_tok=slab_tok):
                self.wt(s, slab_tok)
                spt_box.append(self.dma(s, "ynsp", spill[h], self.yslab[:]))
            self.Y(_)
            self.g_yslab = [spt_box[0]]
            last_t = self.tok(s_t)
            self.g_bias = [self.tok(s_s)]
            self.g_w[buf] = [last_t]
            self.g_tab[buf] = [self.tok(s_v)]
            self.g_q = [last_t]
            self.g_k = [last_t]
        self.attn_done = (self.tok(s_t), self.tok(s_v), self.tok(s_s),
                          self.tok(self.dsem("ynsp")))

    def emit_gate_proj(self, cvv, fill_gate=(), fill_proj=()):
        fill_gate = list(fill_gate)
        fill_proj = list(fill_proj)
        cfg = self.cfg
        CT, OWN = cfg.CT, cfg.OWN
        s_t, s_v, s_s = self.s_t, self.s_v, self.s_s
        pb = self.pb
        ynl = [self.qb_sb, self.ybuf]
        g_yn = [list(self.g_q), list(self.g_ybuf) + list(self.g_yb2)]
        ya0 = self.lntmp
        wload = {}
        for dct in range(CT):
            buf = dct % 2

            def _(g, dct=dct, buf=buf):
                self.wt(g, *self.g_w[buf], *self.mem_guard)
                self.dma(g, f"wh{buf}", self.wq_h[buf].ap(),
                         self.wg_r[:, :, dct * P:(dct + 1) * P])
            self.G(_)
            wload[dct] = self.tok(self.dsem(f"wh{buf}"))
            yld_box = []

            def _(s, dct=dct, buf=buf):
                self.wt(s, *g_yn[buf], *self.g_yslab)
                yld_box.append(self.dma(s, f"ynld{buf}", ynl[buf][:], self.yn_sp_r[dct]))
            self.Y(_)
            yld = yld_box[0]

            for ci, (c0, cw) in enumerate(cfg.own_chunks):
                key = "A" if ci % 2 == 0 else "B"

                def _(t, dct=dct, buf=buf, c0=c0, cw=cw, key=key):
                    self.wt(t, wload[dct], *self.bfree[key])
                    for ct in range(CT):
                        inst = t.matmul(pb[key][:, 0:cw], self.wq_h[buf][:, ct, :],
                                        self.xn_ownT[:, ct, c0:c0 + cw],
                                        start=(ct == 0), stop=(ct == CT - 1))
                    self.inc(inst, s_t)
                self.T(_)
                tg = self.tok(s_t)
                vt = self.vtmp if ci % 2 == 0 else self.vtmp2

                def _(s, cw=cw, key=key, tg=tg, vt=vt):
                    self.wt(s, tg, *self.g_sil)
                    self.inc(s.activation(vt[:, 0:cw], pb[key][:, 0:cw], AF.Silu), s_s)
                self.S(_)
                sil = self.tok(s_s)
                self.bfree[key] = [sil]

                def _(v, dct=dct, buf=buf, c0=c0, cw=cw, sil=sil, yld=yld, vt=vt):
                    self.wt(v, sil, yld)
                    self.inc(v.tensor_scalar(ya0[:, 0:cw], ynl[buf][:, c0:c0 + cw],
                                             cvv["gnw"][:, dct:dct + 1],
                                             cvv["gnb"][:, dct:dct + 1],
                                             op0=ALU.mult, op1=ALU.add), s_v)
                    self.wt(v, self.tok(s_v))
                    self.inc(v.tensor_tensor(self.bufA[:, dct, c0:c0 + cw],
                                             vt[:, 0:cw], ya0[:, 0:cw], op=ALU.mult), s_v)
                self.V(_)
                self.g_sil = [self.tok(s_v)]
            self.g_w[buf] = [tg]
            g_yn[buf] = [self.tok(s_v)]
            npump = (len(fill_gate) + CT - 1 - dct) // (CT - dct)
            for _i in range(min(npump, len(fill_gate))):
                fill_gate.pop(0)()
        for fn in fill_gate:
            fn()
        og_tok = self.tok(s_v)
        self.g_lntmp = [og_tok]
        self.dbg_dump("og", self.bufA, [P, CT * OWN], [og_tok])

        sslb = [(self.kb_sb, 0), (self.kb_sb, OWN)]
        g_ssl = [[], []]
        for ct2 in range(CT):
            buf = ct2 % 2

            def _(g, ct2=ct2, buf=buf):
                self.wt(g, *self.g_w2[buf])
                self.dma(g, f"wk{buf}", self.wk_h[buf].ap(),
                         self.wproj_r[:, :, ct2 * P:(ct2 + 1) * P])
            self.G(_)
            wl = self.tok(self.dsem(f"wk{buf}"))
            for ci, (c0, cw) in enumerate(cfg.own_chunks):
                key = "A" if ci % 2 == 0 else "B"

                def _(t, ct2=ct2, buf=buf, c0=c0, cw=cw, key=key, wl=wl):
                    self.wt(t, wl, og_tok, *self.bfree[key])
                    for dct in range(CT):
                        inst = t.matmul(pb[key][:, 0:cw], self.wk_h[buf][:, dct, :],
                                        self.bufA[:, dct, c0:c0 + cw],
                                        start=(dct == 0), stop=(dct == CT - 1))
                    self.inc(inst, s_t)
                self.T(_)
                tg = self.tok(s_t)

                def _(s, buf=buf, c0=c0, cw=cw, key=key, tg=tg):
                    self.wt(s, tg, *g_ssl[buf])
                    tb, off = sslb[buf]
                    self.inc(s.activation(tb[:, off + c0:off + c0 + cw], pb[key][:, 0:cw],
                                          AF.Identity), s_s)
                self.S(_)
                self.bfree[key] = [self.tok(s_s)]
            sev = self.tok(s_s)

            def _(s, ct2=ct2, buf=buf, sev=sev):
                self.wt(s, sev)
                tb, off = sslb[buf]
                self.dma(s, f"ssp{buf}", self.s_sp_r[ct2], tb[:, off:off + OWN])
            self.Y(_)
            g_ssl[buf] = [self.tok(self.dsem(f"ssp{buf}"))]
            self.g_w2[buf] = [tg]
            npump = (len(fill_proj) + CT - 1 - ct2) // (CT - ct2)
            for _i in range(min(npump, len(fill_proj))):
                fill_proj.pop(0)()
        for fn in fill_proj:
            fn()
        self.proj_tok = self.tok(s_t)
        self.proj_sqd = (self.tok(self.dsem("ssp0")), self.tok(self.dsem("ssp1")))
        self.g_ybuf = [self.tok(s_v)]
        self.g_yb2 = []
        self.g_q = [self.tok(s_v)]
        self.g_k = [*self.proj_sqd]
        self.g_v = [self.tok(s_v)]

    def emit_fmajor_ln(self, src3, dst3, total_width, svec, bvec, tag, ready=()):
        tok = None
        for col0 in range(0, total_width, self.cfg.OWN):
            stats, tail = self.ln_units(src3, dst3, col0,
                                        min(self.cfg.OWN, total_width - col0),
                                        svec, bvec, ready)
            for fn in stats + tail:
                fn()
            tok = self._ln_tok
        return tok

    def ln_units(self, src3, dst3, col0, width, svec, bvec, ready,
                 scr_pair=None, apply_ready=None, rowp=0):
        cfg = self.cfg
        C, CT = cfg.C, cfg.CT
        s_t, s_v, s_s = self.s_t, self.s_v, self.s_s
        pb = self.pb
        OWN = cfg.OWN
        wch = [(col0 + o, w) for (o, w) in chunks(width)]
        srow0, qrow0, mrow0 = 0, OWN, 2 * OWN
        rb = self.rowbuf
        scr = scr_pair or (self.vtmp, self.vtmp2)
        if not hasattr(self, "g_sq_pair"):
            self.g_sq_pair = [list(self.g_sqtmp), list(self.g_sqtmp)]
        sq_guard = self.g_sq_pair
        stats_units = []
        ev_toks = {}

        def stat_unit(ci, c0, cw, ct):
            sqb = scr[ct % 2]

            def _(s, c0=c0, cw=cw, ct=ct, sqb=sqb):
                self.wt(s, *ready, *sq_guard[ct % 2])
                self.inc(s.activation(sqb[:, 0:cw], src3[:, ct, c0:c0 + cw],
                                      AF.Square), s_s)
            self.S(_)
            sqk = self.tok(s_s)

            def _(t, c0=c0, cw=cw, ct=ct, sqk=sqk, sqb=sqb):
                self.wt(t, sqk, *ready, self.consts_sp)
                if ct == 0:
                    self.wt(t, *self.bfree["S"], *self.bfree["C"])
                t.matmul(pb["S"][0:1, 0:cw], self.ones_col[:],
                         src3[:, ct, c0:c0 + cw],
                         start=(ct == 0), stop=(ct == CT - 1))
                inst = t.matmul(pb["C"][0:1, 0:cw], self.ones_col[:],
                                sqb[:, 0:cw],
                                start=(ct == 0), stop=(ct == CT - 1))
                self.inc(inst, s_t)
            self.T(_)
            sq_guard[ct % 2] = [self.tok(s_t)]
            if ct == CT - 1:
                tg = self.tok(s_t)

                def _(v, c0=c0, cw=cw, tg=tg):
                    self.wt(v, tg, self.tok(s_v))
                    self.inc(v.tensor_copy(
                        rb[rowp:rowp + 1, srow0 + c0 - col0:srow0 + c0 - col0 + cw],
                        pb["S"][0:1, 0:cw]), s_v)
                    self.inc(v.tensor_copy(
                        rb[rowp:rowp + 1, qrow0 + c0 - col0:qrow0 + c0 - col0 + cw],
                        pb["C"][0:1, 0:cw]), s_v)
                self.V(_)
                self.bfree["S"] = [self.tok(s_v)]
                self.bfree["C"] = [self.tok(s_v)]
                ev_toks[ci] = self.tok(s_v)

        for ci, (c0, cw) in enumerate(wch):
            for ct in range(CT):
                stats_units.append(lambda ci=ci, c0=c0, cw=cw, ct=ct:
                                   stat_unit(ci, c0, cw, ct))

        srow = rb[rowp:rowp + 1, srow0:srow0 + width]
        qrow = rb[rowp:rowp + 1, qrow0:qrow0 + width]
        mrow = rb[rowp:rowp + 1, mrow0:mrow0 + width]
        rows_box = {}
        tail_units = []

        def rows_unit():
            def _(v):
                self.wt(v, self.tok(s_v))
                self.inc(v.tensor_scalar(srow, srow, 1.0 / C, None, op0=ALU.mult), s_v)
                self.inc(v.tensor_scalar(qrow, qrow, 1.0 / C, None, op0=ALU.mult), s_v)
                self.wt(v, self.tok(s_v))
                self.inc(v.tensor_tensor(mrow, srow, srow, op=ALU.mult), s_v)
                self.wt(v, self.tok(s_v))
                self.inc(v.tensor_tensor(qrow, qrow, mrow, op=ALU.subtract), s_v)
                self.wt(v, self.tok(s_v))
                self.inc(v.tensor_scalar(qrow, qrow, 1e-5, None, op0=ALU.add), s_v)
            self.V(_)

            def _(s):
                self.wt(s, self.tok(s_v))
                self.inc(s.activation(mrow, qrow, AF.Ln), s_s)
                self.wt(s, self.tok(s_s))
                self.inc(s.activation(qrow, mrow, AF.Exp, scale=-0.5), s_s)
            self.S(_)
            rows_box["tok"] = self.tok(s_s)
        tail_units.append(rows_unit)

        def bcast_unit(ci, c0, cw):
            def _(t, c0=c0, cw=cw):
                self.wt(t, rows_box["tok"], *self.bfree["D"], *self.bfree["Y"],
                        self.consts_sp)
                inst = t.matmul(pb["D"][:, 0:cw], self.ones_row_f[rowp:rowp + 1, :],
                                rb[rowp:rowp + 1, srow0 + c0 - col0:srow0 + c0 - col0 + cw],
                                start=True, stop=True)
                inst = t.matmul(pb["Y"][:, 0:cw], self.ones_row_f[rowp:rowp + 1, :],
                                rb[rowp:rowp + 1, qrow0 + c0 - col0:qrow0 + c0 - col0 + cw],
                                start=True, stop=True)
                self.inc(inst, s_t)
            self.T(_)
            rows_box[("bc", ci)] = self.tok(s_t)

        def apply_unit(ci, c0, cw, ct):
            tg = rows_box[("bc", ci)]

            def _(v, c0=c0, cw=cw, ct=ct, tg=tg):
                self.wt(v, tg, *self.g_lntmp, *(apply_ready or []))
                self.inc(v.tensor_tensor(self.lntmp[:, 0:cw], src3[:, ct, c0:c0 + cw],
                                         pb["D"][:, 0:cw], op=ALU.subtract), s_v)
                self.wt(v, self.tok(s_v))
                self.inc(v.tensor_tensor(self.lntmp[:, 0:cw], self.lntmp[:, 0:cw],
                                         pb["Y"][:, 0:cw], op=ALU.mult), s_v)
                self.wt(v, self.tok(s_v))
                self.inc(v.tensor_scalar(dst3[:, ct, c0:c0 + cw], self.lntmp[:, 0:cw],
                                         svec[:, ct:ct + 1], bvec[:, ct:ct + 1],
                                         op0=ALU.mult, op1=ALU.add), s_v)
                self.g_lntmp = [self.tok(s_v)]
            self.V(_)
            if ct == CT - 1:
                self.bfree["D"] = [self.tok(s_v)]
                self.bfree["Y"] = [self.tok(s_v)]
                self._ln_tok = self.tok(s_v)

        for ci, (c0, cw) in enumerate(wch):
            tail_units.append(lambda ci=ci, c0=c0, cw=cw: bcast_unit(ci, c0, cw))
            for ct in range(CT):
                tail_units.append(lambda ci=ci, c0=c0, cw=cw, ct=ct:
                                  apply_unit(ci, c0, cw, ct))
        return stats_units, tail_units

    def emit_wo_x2_ln2(self, cvv):
        cfg = self.cfg
        CT, OWN, H = cfg.CT, cfg.OWN, cfg.H
        s_t, s_v, s_s = self.s_t, self.s_v, self.s_s
        pb = self.pb
        attn_t = self.attn_done[0]
        ctx_toks = []
        for h in range(H):
            def _(s, h=h):
                self.wt(s, attn_t, self.attn_done[3])
                ctx_toks.append(self.dma(s, f"ctxld{h % 4}",
                                         self.xn_allT[:, h, 0:OWN], self.ctx_sp_r[h]))
            self.Y(_)
        ctx_ld_all = [self.tok(self.dsem(f"ctxld{j}")) for j in range(min(4, H))]

        x2l = [self.qb_sb, self.ybuf]
        g_x2 = [[attn_t], [attn_t]]
        nch = len(cfg.own_chunks)
        sum_banks = ["S", "D"]
        sq_banks = ["C", "Y"]
        for ct2 in range(CT):
            buf = ct2 % 2

            def _(g, ct2=ct2, buf=buf):
                self.wt(g, *self.g_w[buf])
                self.dma(g, f"wh{buf}", self.wq_h[buf].ap(),
                         self.woT_r[:, :, ct2 * P:(ct2 + 1) * P])
            self.G(_)
            wl = self.tok(self.dsem(f"wh{buf}"))
            for ci, (c0, cw) in enumerate(cfg.own_chunks):
                key = "A" if ci % 2 == 0 else "B"

                def _(t, ct2=ct2, buf=buf, c0=c0, cw=cw, key=key, wl=wl):
                    self.wt(t, wl, *ctx_ld_all, *self.bfree[key])
                    for hh in range(CT):
                        inst = t.matmul(pb[key][:, 0:cw], self.wq_h[buf][:, hh, :],
                                        self.xn_allT[:, hh, c0:c0 + cw],
                                        start=(hh == 0), stop=(hh == CT - 1))
                    self.inc(inst, s_t)
                self.T(_)
                tg = self.tok(s_t)

                def _(v, ct2=ct2, buf=buf, c0=c0, cw=cw, key=key, tg=tg):
                    self.wt(v, tg, *g_x2[buf])
                    self.inc(v.tensor_tensor(x2l[buf][:, c0:c0 + cw], pb[key][:, 0:cw],
                                             self.xn_ownT[:, ct2, c0:c0 + cw],
                                             op=ALU.add), s_v)
                self.V(_)
                x2tok = self.tok(s_v)
                self.bfree[key] = [x2tok]

                def _(s, buf=buf, c0=c0, cw=cw, x2tok=x2tok):
                    self.wt(s, x2tok, *self.g_sqtmp)
                    self.inc(s.activation(self.sqtmp[:, 0:cw], x2l[buf][:, c0:c0 + cw],
                                          AF.Square), s_s)
                self.S(_)
                sqk = self.tok(s_s)

                def _(t, ct2=ct2, buf=buf, c0=c0, cw=cw, ci=ci, sqk=sqk):
                    self.wt(t, sqk, self.consts_sp)
                    if ct2 == 0:
                        self.wt(t, *self.bfree[sum_banks[ci]], *self.bfree[sq_banks[ci]])
                    t.matmul(pb[sum_banks[ci]][0:1, 0:cw], self.ones_col[:],
                             x2l[buf][:, c0:c0 + cw],
                             start=(ct2 == 0), stop=(ct2 == CT - 1))
                    inst = t.matmul(pb[sq_banks[ci]][0:1, 0:cw], self.ones_col[:],
                                    self.sqtmp[:, 0:cw],
                                    start=(ct2 == 0), stop=(ct2 == CT - 1))
                    self.inc(inst, s_t)
                self.T(_)
                self.g_sqtmp = [self.tok(s_t)]
            stat_t = self.tok(s_t)

            def _(s, ct2=ct2, buf=buf, stat_t=stat_t):
                self.wt(s, stat_t)
                self.dma(s, f"x2sp{buf}", self.x2_sp_r[ct2], x2l[buf][:])
            self.Y(_)
            g_x2[buf] = [self.tok(self.dsem(f"x2sp{buf}"))]
            self.g_w[buf] = [stat_t]
        stats_done = self.tok(s_t)

        width = OWN
        rb = self.rowbuf
        for ci, (c0, cw) in enumerate(cfg.own_chunks):
            def _(v, ci=ci, c0=c0, cw=cw):
                self.wt(v, stats_done)
                self.inc(v.tensor_copy(rb[0:1, c0:c0 + cw],
                                       pb[sum_banks[ci]][0:1, 0:cw]), s_v)
                self.inc(v.tensor_copy(rb[0:1, OWN + c0:OWN + c0 + cw],
                                       pb[sq_banks[ci]][0:1, 0:cw]), s_v)
            self.V(_)
            self.bfree[sum_banks[ci]] = [self.tok(s_v)]
            self.bfree[sq_banks[ci]] = [self.tok(s_v)]
        srow = rb[0:1, 0:width]
        qrow = rb[0:1, OWN:OWN + width]
        mrow = rb[0:1, 2 * OWN:2 * OWN + width]

        def _(v):
            self.wt(v, self.tok(s_v))
            self.inc(v.tensor_scalar(srow, srow, 1.0 / cfg.C, None, op0=ALU.mult), s_v)
            self.inc(v.tensor_scalar(qrow, qrow, 1.0 / cfg.C, None, op0=ALU.mult), s_v)
            self.wt(v, self.tok(s_v))
            self.inc(v.tensor_tensor(mrow, srow, srow, op=ALU.mult), s_v)
            self.wt(v, self.tok(s_v))
            self.inc(v.tensor_tensor(qrow, qrow, mrow, op=ALU.subtract), s_v)
            self.wt(v, self.tok(s_v))
            self.inc(v.tensor_scalar(qrow, qrow, 1e-5, None, op0=ALU.add), s_v)
        self.V(_)

        def _(s):
            self.wt(s, self.tok(s_v))
            self.inc(s.activation(mrow, qrow, AF.Ln), s_s)
            self.wt(s, self.tok(s_s))
            self.inc(s.activation(qrow, mrow, AF.Exp, scale=-0.5), s_s)
        self.S(_)
        rows_tok = self.tok(s_s)

        bcast_done = {}
        for ci, (c0, cw) in enumerate(cfg.own_chunks):
            def _(t, ci=ci, c0=c0, cw=cw):
                self.wt(t, rows_tok, *self.bfree[sum_banks[ci]], *self.bfree[sq_banks[ci]])
                t.matmul(pb[sum_banks[ci]][:, 0:cw], self.ones_row_f[0:1, :],
                         srow[:, c0:c0 + cw], start=True, stop=True)
                inst = t.matmul(pb[sq_banks[ci]][:, 0:cw], self.ones_row_f[0:1, :],
                                qrow[:, c0:c0 + cw], start=True, stop=True)
                self.inc(inst, s_t)
            self.T(_)
            bcast_done[ci] = self.tok(s_t)
        for ct2 in range(CT):
            buf = ct2 % 2

            xld_box = []

            def _(s, ct2=ct2, buf=buf):
                self.wt(s, *g_x2[buf])
                xld_box.append(self.dma(s, f"x2ld{buf}", x2l[buf][:], self.x2_sp_r[ct2]))
            self.Y(_)
            xld = xld_box[0]
            for ci, (c0, cw) in enumerate(cfg.own_chunks):
                def _(v, ct2=ct2, buf=buf, ci=ci, c0=c0, cw=cw, xld=xld):
                    self.wt(v, xld, bcast_done[ci], *self.g_lntmp, attn_t)
                    self.inc(v.tensor_tensor(self.lntmp[:, 0:cw], x2l[buf][:, c0:c0 + cw],
                                             pb[sum_banks[ci]][:, 0:cw], op=ALU.subtract), s_v)
                    self.wt(v, self.tok(s_v))
                    self.inc(v.tensor_tensor(self.lntmp[:, 0:cw], self.lntmp[:, 0:cw],
                                             pb[sq_banks[ci]][:, 0:cw], op=ALU.mult), s_v)
                    self.wt(v, self.tok(s_v))
                    self.inc(v.tensor_scalar(self.bufA[:, ct2, c0:c0 + cw],
                                             self.lntmp[:, 0:cw],
                                             cvv["ln2s"][:, ct2:ct2 + 1],
                                             cvv["ln2b"][:, ct2:ct2 + 1],
                                             op0=ALU.mult, op1=ALU.add), s_v)
                    self.g_lntmp = [self.tok(s_v)]
                self.V(_)
            g_x2[buf] = [self.tok(s_v)]
        for ci in range(len(cfg.own_chunks)):
            self.bfree[sum_banks[ci]] = [self.tok(s_v)]
            self.bfree[sq_banks[ci]] = [self.tok(s_v)]
        self.x2n_tok = self.tok(s_v)
        self.g_x2l = g_x2
        self.dbg_dump("x2nT", self.bufA, [P, CT * OWN], [self.x2n_tok])

    def emit_ff_stageA(self, hg_sp_w):
        cfg = self.cfg
        CT, FT = cfg.CT, cfg.FT
        s_t, s_s = self.s_t, self.s_s
        pb = self.pb
        hgt = [self.qb_sb, self.ybuf]
        g_hg = [list(self.g_x2l[0]), list(self.g_x2l[1])]
        for ft in range(FT):
            buf = ft % 2

            def _(g, ft=ft, buf=buf):
                self.wt(g, *self.g_w[buf], self.pc["ff1"])
                self.dma(g, f"wh{buf}", self.wq_h[buf].ap(),
                         self.ff1Tbf_r[:, :, ft * P:(ft + 1) * P])
            self.G(_)
            wl = self.tok(self.dsem(f"wh{buf}"))
            for ci, (c0, cw) in enumerate(cfg.own_chunks):
                key = "A" if ci % 2 == 0 else "B"

                def _(t, ft=ft, buf=buf, c0=c0, cw=cw, key=key, wl=wl):
                    self.wt(t, wl, self.x2n_tok, *self.bfree[key])
                    for ct in range(CT):
                        inst = t.matmul(pb[key][:, 0:cw], self.wq_h[buf][:, ct, :],
                                        self.bufA[:, ct, c0:c0 + cw],
                                        start=(ct == 0), stop=(ct == CT - 1))
                    self.inc(inst, s_t)
                self.T(_)
                tg = self.tok(s_t)

                def _(s, buf=buf, c0=c0, cw=cw, key=key, tg=tg):
                    self.wt(s, tg, *g_hg[buf])
                    self.inc(s.activation(hgt[buf][:, c0:c0 + cw], pb[key][:, 0:cw],
                                          AF.Gelu_apprx_tanh), s_s)
                self.S(_)
                self.bfree[key] = [self.tok(s_s)]
                self.g_w[buf] = [tg]
            hev = self.tok(s_s)

            def _(s, ft=ft, buf=buf, hev=hev):
                self.wt(s, hev)
                self.dma(s, f"hgsp{buf}", hg_sp_w[ft], hgt[buf][:])
            self.Y(_)
            g_hg[buf] = [self.tok(self.dsem(f"hgsp{buf}"))]
        self.stageA_t = self.tok(s_t)
        self.stageA_sqd = (self.tok(self.dsem("hgsp0")), self.tok(self.dsem("hgsp1")))

    def emit_stageB_combine(self, hg_sp_r, cvv, xown_r, outown_r):
        cfg = self.cfg
        CT, OWN, FT, C, R = cfg.CT, cfg.OWN, cfg.FT, cfg.C, cfg.R
        s_t, s_v, s_s = self.s_t, self.s_v, self.s_s
        pb = self.pb
        sb_es = ExitStack()
        wff2 = [self.sbuf(f"wff2_{i}", [P, FT, P], BF16, es=sb_es) for i in range(2)]
        sgch = self.sbuf("sgch", [P, CT, 512], BF16, es=sb_es)
        sgT = self.sbuf("sgT", [P, C], BF16, es=sb_es)
        cb_t0 = self.sbuf("cb_t0", [P, 512], BF16, es=sb_es)
        cb_t1 = self.sbuf("cb_t1", [P, 512], BF16, es=sb_es)
        cb_t2 = self.sbuf("cb_t2", [P, 512], BF16, es=sb_es)
        cb_g = self.sbuf("cb_g", [P, 512], BF16, es=sb_es)
        cb_gs = self.sbuf("cb_gs", [P, 512], BF16, es=sb_es)
        x2sl = [self.sbuf(f"x2sl{i}", [P, 512], BF16, es=sb_es) for i in range(2)]
        ssl = [self.sbuf(f"ssl{i}", [P, 512], BF16, es=sb_es) for i in range(2)]
        mrows = self.sbuf("mrows", [P, 3 * 512], BF16, es=sb_es)
        mb = [(mrows, 0), (mrows, 512), (mrows, 1024)]
        xa2 = self.sbuf("xa2", [P, C], F32, es=sb_es)
        outb1 = self.sbuf("outb1", [P, C], F32, es=sb_es)
        wupT_sb = self.sbuf("wupT_sb", [R, C], BF16, es=sb_es)
        self.wupT_sb = wupT_sb
        self.sb_es = sb_es

        def _(g):
            self.wt(g, *self.mem_guard)
            self.dma(g, "gconsts", wupT_sb[:], self.w_upT_d[:])
        self.G(_)
        wup_tok = self.tok(self.dsem("gconsts"))
        hgflat = self.xn_allT.ap().rearrange("p a n -> p (a n)")
        g_wff2 = [[], []]
        g_ssl = [[self.proj_sqd[0]], [self.proj_sqd[1]]]
        g_x2sl = [[], []]
        g_sgch = []
        g_cb = []
        g_sgT = []
        g_out = [[], []]
        g_hgbuf = [self.stageA_t, self.attn_done[0]]

        sg_dbg = None
        if "sg" in self.dbg:
            sg_dbg = self.nc.dram_tensor("dbg_sg", [OWN, C], F32,
                                         kind="ExternalOutput").ap() \
                .rearrange("(rt p) c -> rt p c", p=P)
            self.out_names.append("dbg_sg")
        for ci, (c0, cw) in enumerate(cfg.own_chunks):
            hg3 = hgflat[:, 0:FT * cw].rearrange("p (a n) -> p a n", n=cw)

            hg_box = []

            def _(s, ci=ci, c0=c0, cw=cw, hg3=hg3):
                self.wt(s, *self.stageA_sqd, *g_hgbuf, *self.mem_guard)
                hg_box.append(self.dma(s, "hgld", hg3, hg_sp_r[:, :, c0:c0 + cw]))
            self.Y(_)
            hg_ld = hg_box[0]

            mk_toks = []
            for j in range(2):
                def _(t, j=j, ci=ci, c0=c0, cw=cw):
                    self.wt(t, self.masks_tok, self.consts_sp, *self.bfree["D"])
                    inst = t.matmul(pb["D"][:, 0:cw], self.ones_row_bf[:],
                                    self.masks_bf[0:1, j * OWN + c0:j * OWN + c0 + cw],
                                    start=True, stop=True)
                    self.inc(inst, s_t)
                self.T(_)
                tg = self.tok(s_t)

                def _(v, cw=cw, tg=tg, j=j):
                    self.wt(v, tg)
                    tb, off = mb[j]
                    self.inc(v.tensor_copy(tb[:, off:off + cw], pb["D"][:, 0:cw]), s_v)
                self.V(_)
                self.bfree["D"] = [self.tok(s_v)]
                mk_toks.append(self.tok(s_v))
            def _(v, cw=cw):
                self.wt(v, *mk_toks)
                self.inc(v.tensor_tensor(mb[2][0][:, mb[2][1]:mb[2][1] + cw],
                                         mb[0][0][:, mb[0][1]:mb[0][1] + cw],
                                         mb[1][0][:, mb[1][1]:mb[1][1] + cw],
                                         op=ALU.add), s_v)
                self.wt(v, self.tok(s_v))
                self.inc(v.tensor_scalar(mb[2][0][:, mb[2][1]:mb[2][1] + cw],
                                         mb[2][0][:, mb[2][1]:mb[2][1] + cw],
                                         -1.0, 1.0, op0=ALU.mult, op1=ALU.add), s_v)
            self.V(_)
            mk_toks.append(self.tok(s_v))

            for ct2 in range(CT):
                buf = ct2 % 2

                wl_q = []

                def _(g, ct2=ct2, buf=buf):
                    self.wt(g, *g_wff2[buf], *self.mem_guard, self.pc["ff2"])
                    wl_q.append(self.dma(g, f"wf2{buf}", wff2[buf][:, :, :],
                                         self.ff2Tbf_r[:, :, ct2 * P:(ct2 + 1) * P]))
                self.G(_)

                sl_box = []

                def _(s, ct2=ct2, buf=buf, c0=c0, cw=cw):
                    self.wt(s, *g_ssl[buf], *self.mem_guard)
                    sl_box.append(self.dma(s, f"sld{buf}", ssl[buf][:, 0:cw],
                                           self.s_sp_r[ct2][:, c0:c0 + cw]))
                    self.wt(s, *g_x2sl[buf])
                    sl_box.append(self.dma(s, f"x2d{buf}", x2sl[buf][:, 0:cw],
                                           self.x2_sp_r[ct2][:, c0:c0 + cw]))
                self.Y(_)
                sl_ld = sl_box

                key = "A" if ct2 % 2 == 0 else "B"

                def _(t, ct2=ct2, buf=buf, c0=c0, cw=cw, key=key, wl_q=wl_q,
                      hg_ld=hg_ld, hg3=hg3):
                    self.wt(t, hg_ld, *self.bfree[key], wl_q[0])
                    for ft in range(FT):
                        inst = t.matmul(pb[key][:, 0:cw], wff2[buf][:, ft, :],
                                        hg3[:, ft, 0:cw], start=(ft == 0),
                                        stop=(ft == FT - 1))
                    self.inc(inst, s_t)
                    self.wt(t, *self.bfree["S"], wup_tok)
                    inst = t.matmul(pb["S"][:, 0:cw],
                                    self.wupT_sb[0:R, ct2 * P:(ct2 + 1) * P],
                                    self.lr_sb[0:R, c0:c0 + cw], start=True, stop=True)
                    self.inc(inst, s_t)
                self.T(_)
                ff_t = self.tok(s_t)
                g_wff2[buf] = [ff_t]

                def _(s, ct2=ct2, c0=c0, cw=cw):
                    self.wt(s, *g_cb)
                    self.inc(s.activation(cb_t0[:, 0:cw], self.xn_ownT[:, ct2, c0:c0 + cw],
                                          AF.Identity,
                                          bias=cvv["sbias"][:, ct2:ct2 + 1]), s_s)
                self.S(_)
                t0_tok = self.tok(s_s)

                def _(v, ct2=ct2, buf=buf, c0=c0, cw=cw, key=key, ff_t=ff_t,
                      sl_ld=sl_ld, t0_tok=t0_tok):
                    self.wt(v, ff_t, *sl_ld, t0_tok, *mk_toks, *g_cb)
                    self.inc(v.tensor_tensor(cb_t1[:, 0:cw],
                                             self.xn_ownT[:, ct2, c0:c0 + cw],
                                             pb["S"][:, 0:cw], op=ALU.add), s_v)
                    self.inc(v.tensor_tensor(cb_t2[:, 0:cw], x2sl[buf][:, 0:cw],
                                             pb[key][:, 0:cw], op=ALU.add), s_v)
                    self.wt(v, self.tok(s_v))
                    self.inc(v.tensor_tensor(cb_g[:, 0:cw], cb_t0[:, 0:cw],
                                             mb[0][0][:, mb[0][1]:mb[0][1] + cw],
                                             op=ALU.mult), s_v)
                    self.inc(v.tensor_tensor(cb_gs[:, 0:cw], cb_t1[:, 0:cw],
                                             mb[1][0][:, mb[1][1]:mb[1][1] + cw],
                                             op=ALU.mult), s_v)
                    self.wt(v, self.tok(s_v))
                    self.inc(v.tensor_tensor(cb_t0[:, 0:cw], cb_t2[:, 0:cw],
                                             mb[2][0][:, mb[2][1]:mb[2][1] + cw],
                                             op=ALU.mult), s_v)
                    self.inc(v.tensor_tensor(cb_g[:, 0:cw], cb_g[:, 0:cw],
                                             cb_gs[:, 0:cw], op=ALU.add), s_v)
                    self.wt(v, self.tok(s_v))
                    self.inc(v.tensor_tensor(cb_g[:, 0:cw], cb_g[:, 0:cw],
                                             cb_t0[:, 0:cw], op=ALU.add), s_v)
                self.V(_)
                gv = self.tok(s_v)
                self.bfree[key] = [gv]
                self.bfree["S"] = [gv]
                g_x2sl[buf] = [gv]

                def _(s, gv=gv, cw=cw):
                    self.wt(s, gv)
                    self.inc(s.activation(cb_t1[:, 0:cw], cb_g[:, 0:cw], AF.Silu), s_s)
                self.S(_)
                gs_tok = self.tok(s_s)

                def _(v, ct2=ct2, buf=buf, cw=cw, gs_tok=gs_tok):
                    self.wt(v, gs_tok, *g_sgch)
                    self.inc(v.tensor_tensor(sgch[:, ct2, 0:cw], cb_t1[:, 0:cw],
                                             ssl[buf][:, 0:cw], op=ALU.mult), s_v)
                self.V(_)
                g_cb = [self.tok(s_v)]
                g_ssl[buf] = [self.tok(s_v)]
            sg_tok = self.tok(s_v)
            g_hgbuf = [self.tok(s_t)]

            for rl in range(cw // P):
                rt = (c0 + rl * P) // P
                ob = outb1
                for g0 in range(0, CT, 8):
                    nk = min(8, CT - g0)
                    tkey = "T" if (g0 // 8) % 2 == 0 else "T2"

                    def _(t, rl=rl, g0=g0, nk=nk, tkey=tkey, sg_tok=sg_tok):
                        self.wt(t, sg_tok, *self.bfree[tkey])
                        for k in range(nk):
                            ct2 = g0 + k
                            inst = t.transpose(pb[tkey][:, k * P:(k + 1) * P],
                                               sgch[:, ct2, rl * P:(rl + 1) * P],
                                               self.eye_sb[:])
                        self.inc(inst, s_t)
                    self.T(_)
                    tg = self.tok(s_t)

                    def _(v, g0=g0, nk=nk, tkey=tkey, tg=tg, rl=rl):
                        self.wt(v, tg, *(g_out[0]), *g_sgT)
                        self.inc(v.tensor_copy(sgT[:, g0 * P:(g0 + nk) * P],
                                               pb[tkey][:, 0:nk * P]), s_v)
                    self.V(_)
                    self.bfree[tkey] = [self.tok(s_v)]
                tp_tok = self.tok(s_v)
                g_sgch = [tp_tok]
                if sg_dbg is not None:
                    def _(g, rt=rt, tp_tok=tp_tok):
                        self.wt(g, tp_tok)
                        self.dbg_tok = self.dma(g, "dbg", sg_dbg[rt], sgT[:])
                    self.G(_)
                    self.dbg_guard = [self.dbg_tok]

                xo_box = []

                def _(s, rt=rt, rl=rl):
                    self.wt(s, *(g_out[0]), *self.mem_guard)
                    xo_box.append(self.dma(s, "xo", xa2[:], xown_r[rt]))
                self.Y(_)
                xld = xo_box[0]

                def _(v, rl=rl, tp_tok=tp_tok, xld=xld, ob=ob):
                    self.wt(v, tp_tok, xld, *(g_out[0]))
                    self.inc(v.tensor_tensor(ob[:], xa2[:], sgT[:],
                                             op=ALU.add), s_v)
                self.V(_)
                ov = self.tok(s_v)
                g_sgT = [ov]

                def _(s, rt=rt, ov=ov, ob=ob, rl=rl):
                    self.wt(s, ov)
                    self.dma(s, "out", outown_r[rt], ob[:])
                self.Y(_)
                g_out[0] = [self.tok(self.dsem("out")), ov]



def build_program(cfg, dbg=()):
    b = Builder(cfg, dbg)
    b.build()
    return b


def host_inputs(cfg, inputs, half):
    f32c = lambda a: np.ascontiguousarray(np.asarray(a, np.float32))
    wq = f32c(inputs["L0_ffn_wq"]); wk = f32c(inputs["L0_ffn_wk"])
    wv = f32c(inputs["L0_ffn_wv"]); wo = f32c(inputs["L0_ffn_wo"])
    m = {
        "w_qkv": f32c(inputs["L0_attn_w_qkv"]),
        "w_g": f32c(inputs["L0_attn_w_g"]),
        "w_proj": f32c(inputs["L0_attn_w_proj"]),
        "wqkvT_f": np.ascontiguousarray(
            np.concatenate([wq.T, wk.T, wv.T], axis=1)),
        "woT": np.ascontiguousarray(wo.T),
        "ff1T": np.ascontiguousarray(f32c(inputs["L0_ffn_w_ff1"]).T),
        "ff2T": np.ascontiguousarray(f32c(inputs["L0_ffn_w_ff2"]).T),
        "w_downT": np.ascontiguousarray(f32c(inputs["L0_ffn_w_down"]).T),
        "w_upT": np.ascontiguousarray(f32c(inputs["L0_ffn_w_up"]).T),
        "wrT": np.ascontiguousarray(f32c(inputs["L0_ffn_w_router"]).T),
        "cvecs": np.concatenate(
            [cvec(inputs["L0_attn_gn_weight"], cfg),
             cvec(inputs["L0_attn_gn_bias"], cfg),
             cvec(inputs["L0_ffn_ln1_scale"], cfg),
             cvec(inputs["L0_ffn_ln1_bias"], cfg),
             cvec(inputs["L0_ffn_ln2_scale"], cfg),
             cvec(inputs["L0_ffn_ln2_bias"], cfg),
             cvec(inputs["L0_ffn_simple_bias"], cfg)], axis=1),
    }
    cq, sq, ck, sk = xpos_tables(cfg, half)
    ltri, ltriT, eye, pmatT = host_consts(cfg, half)
    m.update(cq=cq, sq=sq, ck=ck, sk=sk, ltri=ltri, ltriT=ltriT, eye=eye,
             pmatT=pmatT)
    return m


def core_inputs(cfg, inputs, shared, core):
    b, half = core // 2, core % 2
    x = np.asarray(inputs["L0_attn_x"], np.float32)[b]
    m = dict(shared[half])
    m["x_all"] = np.ascontiguousarray(x)
    m["x_own"] = np.ascontiguousarray(x[half::2, :])
    return m


def kernel(**inputs):
    cfg = CFG()
    bld = build_program(cfg)
    shared = [host_inputs(cfg, inputs, half) for half in range(2)]
    in_maps = [core_inputs(cfg, inputs, shared, c) for c in range(8)]
    res = run_bass_kernel_spmd(bld.nc, in_maps, core_ids=list(range(8)))
    out = np.empty((cfg.NB, cfg.T, cfg.C), np.float32)
    for c in range(8):
        b, half = c // 2, c % 2
        out[b, half::2, :] = res.results[c]["out_own"]
    return out
